# version 03
# baseline (speedup 1.0000x reference)
import numpy as np
import concourse.bass as bass
import concourse.mybir as mybir
from concourse.bass_utils import run_bass_kernel_spmd
from concourse.bass_core_rust import IndirectOffsetOnAxis
from contextlib import ExitStack

F32 = mybir.dt.float32
BF16 = mybir.dt.bfloat16
U32 = mybir.dt.uint32
ALU = mybir.AluOpType
ACT = mybir.ActivationFunctionType
AX = mybir.AxisListType

D = 1024
NT = 16
TPC = 2048
NCH = 16
NE = 8
HF = 4096
CAP = 640
NST = CAP // 128

ENGS = ("tensor", "vector", "scalar", "gpsimd", "sync")


_NO_DRAIN = {"wait_ge", "drain", "nop", "dma_start", "then_inc", "sem_inc",
             "collective_compute", "indirect_dma_start", "memset"}


class DrainWrap:

    def __init__(self, h):
        self._h = h

    def __getattr__(self, name):
        v = getattr(self._h, name)
        if callable(v) and not name.startswith("_") and name not in _NO_DRAIN:
            def wrapped(*a, **k):
                self._h.drain()
                return v(*a, **k)
            return wrapped
        return v



class Prog:

    def __init__(self, nc, ctx):
        self.nc = nc
        self.segs = [{g: [] for g in ENGS}]
        self.colls = []
        self.sem = {g: ctx.enter_context(nc.semaphore("sem_" + g)) for g in ENGS}
        self.count = {g: 0 for g in ENGS}
        self.labels = {}

    def op(self, eng, fn, after=(), label=None, dma=False):
        waits = []
        for a in after:
            src, val, srcdma = self.labels[a]
            if src != eng or srcdma:
                waits.append((src, val))
        if label is not None:
            self.count[eng] += 16 if dma else 1
            self.labels[label] = (eng, self.count[eng], dma)
        self.segs[-1][eng].append(dict(fn=fn, waits=waits, label=label, dma=dma))

    def collective(self, fn, after=(), label=None):
        waits = []
        for a in after:
            src, val, srcdma = self.labels[a]
            if src != "gpsimd" or srcdma:
                waits.append((src, val))
        if label is not None:
            self.count["gpsimd"] += 1
            self.labels[label] = ("gpsimd", self.count["gpsimd"], False)
        self.colls.append(dict(fn=fn, waits=waits, label=label))
        self.segs.append({g: [] for g in ENGS})

    def emit(self, blk):
        engmap = {"tensor": blk.tensor, "vector": blk.vector, "scalar": blk.scalar,
                  "gpsimd": blk.gpsimd, "sync": blk.sync}
        hiwater = {g: {gg: 0 for gg in ENGS} for g in ENGS}

        def make(eng_ops, eng_name):
            def body(h):
                hh = DrainWrap(h) if eng_name in ("vector", "scalar") else h
                hi = hiwater[eng_name]
                for o in eng_ops:
                    for (src, val) in o["waits"]:
                        if val > hi[src]:
                            h.wait_ge(self.sem[src], val)
                            hi[src] = val
                    inst = o["fn"](hh)
                    if o["label"] is not None:
                        assert inst is not None, f"label {o['label']} lacks inst"
                        if not o["dma"] and eng_name in ("vector", "scalar"):
                            inst = h.drain()
                        inst.then_inc(self.sem[eng_name], 16 if o["dma"] else 1)
            return body

        for si, seg in enumerate(self.segs):
            for eng in ENGS:
                if seg[eng]:
                    engmap[eng](make(seg[eng], eng))
            if si < len(self.colls):
                co = self.colls[si]

                def cbody(h, co=co):
                    hi = hiwater["gpsimd"]
                    for (src, val) in co["waits"]:
                        if val > hi[src]:
                            h.wait_ge(self.sem[src], val)
                            hi[src] = val
                    inst = co["fn"](h)
                    if co["label"] is not None:
                        inst.then_inc(self.sem["gpsimd"], 1)
                engmap["gpsimd"](cbody)


def build(dbg=False, stage="A"):
    nc = bass.Bass("TRN2", target_bir_lowering=False, debug=False)

    d_ccin_s = nc.dram_tensor("ccin_s", [128, 8], F32, kind="Internal").ap()
    d_ccout_s = nc.dram_tensor("ccout_s", [128, 8], F32, kind="Internal",
                               addr_space="Shared").ap()
    d_xt = nc.dram_tensor("xt_in", [17 * 128, D], F32, kind="ExternalInput").ap()
    d_convw = nc.dram_tensor("convw", [D, 4], F32, kind="ExternalInput").ap()
    d_convb = nc.dram_tensor("convb", [D, 1], F32, kind="ExternalInput").ap()
    d_win = nc.dram_tensor("win", [D, 80], F32, kind="ExternalInput").ap()
    d_bin = nc.dram_tensor("bin", [1, 80], F32, kind="ExternalInput").ap()
    d_wout = nc.dram_tensor("wout", [NCH, D], F32, kind="ExternalInput").ap()
    d_bout = nc.dram_tensor("bout", [1, D], F32, kind="ExternalInput").ap()
    d_alog = nc.dram_tensor("alog", [NCH, 1], F32, kind="ExternalInput").ap()
    d_gatew = nc.dram_tensor("gatew", [D, NE], F32, kind="ExternalInput").ap()
    d_cvec = nc.dram_tensor("cvec", [1, D], F32, kind="ExternalInput").ap()
    d_wmod = nc.dram_tensor("wmod", [D, 6 * D], F32, kind="ExternalInput").ap()
    d_w1 = nc.dram_tensor("w1", [NE * D, HF], F32, kind="ExternalInput").ap()
    d_w2 = nc.dram_tensor("w2", [NE * HF, D], F32, kind="ExternalInput").ap()
    d_consts = nc.dram_tensor("consts", [1, 8], F32, kind="ExternalInput").ap()
    d_onehot = nc.dram_tensor("onehot", [NCH, 8], F32, kind="ExternalInput").ap()
    d_t16 = nc.dram_tensor("t16", [NCH, 128], F32, kind="ExternalInput").ap()
    d_rowm = nc.dram_tensor("rowm", [128, 1], F32, kind="ExternalInput").ap()
    d_cctest = nc.dram_tensor("cctest", [128, 8], F32, kind="ExternalInput").ap()
    d_t8 = nc.dram_tensor("t8", [NE, 64], F32, kind="ExternalInput").ap()
    d_rowm8 = nc.dram_tensor("rowm8", [64, 1], F32, kind="ExternalInput").ap()
    d_i8 = nc.dram_tensor("i8", [NE, NE], F32, kind="ExternalInput").ap()
    d_ones8 = nc.dram_tensor("ones8", [NE, 128], F32, kind="ExternalInput").ap()
    d_esel = nc.dram_tensor("esel", [128, NE], F32, kind="ExternalInput").ap()
    d_umat = nc.dram_tensor("umat", [128, 128], F32, kind="ExternalInput").ap()
    d_gmat = nc.dram_tensor("gmat", [128, 128], F32, kind="ExternalInput").ap()

    d_y = nc.dram_tensor("y", [2049, D], F32, kind="ExternalOutput").ap()

    d_ccin_p = nc.dram_tensor("ccin_p", [8 * NE, TPC], F32, kind="Internal").ap()
    d_ccout_p = nc.dram_tensor("ccout_p", [8 * NE, TPC], F32, kind="Internal",
                               addr_space="Shared").ap()
    d_rowscr = nc.dram_tensor("rowscr", [3, D], F32, kind="Internal").ap()
    d_hfsp = nc.dram_tensor("hfsp", [2049, D], BF16, kind="Internal").ap()
    d_ysel = nc.dram_tensor("ysel", [NE * CAP, D], BF16, kind="Internal").ap()
    d_w2bf = nc.dram_tensor("w2bf", [NE * HF, D], BF16, kind="Internal").ap()

    dbg_out = {}
    if dbg == "c":
        def mkdbgc(name, shape, dt=F32):
            dbg_out[name] = nc.dram_tensor("dbg_" + name, shape, dt,
                                           kind="ExternalOutput").ap()
        mkdbgc("tau8", [NE, 1])
        mkdbgc("pbis", [128, 1024])
        mkdbgc("cnt", [128, 1])
        mkdbgc("probs", [128, NT * NE])
        mkdbgc("mask", [128, NT * NE])
        mkdbgc("pose", [128, NE * NT])
        mkdbgc("psel", [128, 8])
        mkdbgc("xinT", [128, CAP])
        mkdbgc("h1d", [128, CAP])
        mkdbgc("ysl", [128, D])
        mkdbgc("ysb", [128, D])
        mkdbgc("hfb", [128, NT * D])
        mkdbgc("ptd", [128, CAP])
    ctx = ExitStack()
    with ctx:
        e = ctx.enter_context
        if dbg is True:
            def mkdbg(name, shape):
                dbg_out[name] = nc.dram_tensor("dbg_" + name, shape, F32,
                                               kind="ExternalOutput").ap()
            mkdbg("projT", [80, TPC])
            mkdbg("abar", [NCH, TPC])
            mkdbg("bbar", [NCH, TPC])
            mkdbg("hsc", [NCH, TPC])
            mkdbg("mod", [1, 6 * D])
            mkdbg("hT0", [128, 2051])
            mkdbg("lnw0", [128, D])
            mkdbg("mv0", [128, 2])
            mkdbg("rn0", [128, 2])
            mkdbg("Ae", [NCH, 8])
            mkdbg("He", [NCH, 8])
            mkdbg("carry", [NCH, 1])
            mkdbg("summ", [NCH, 8])
            mkdbg("ccraw", [128, 8])
            mkdbg("ccin", [128, 8])

        ident = e(nc.sbuf_tensor("ident", [128, 128], F32))
        cap_t = e(nc.sbuf_tensor("cap_t", [128, 4], F32))
        gbrep = e(nc.sbuf_tensor("gbrep", [128, D], F32))
        stats = e(nc.sbuf_tensor("stats", [128, 12], F32))
        mv = e(nc.sbuf_tensor("mv", [128, 2], F32))
        mv2 = e(nc.sbuf_tensor("mv2", [128, 2], F32))
        mvd = e(nc.sbuf_tensor("mvd", [128, 1], F32))
        stdt = e(nc.sbuf_tensor("stdt", [128, 1], F32))
        rstd = e(nc.sbuf_tensor("rstd", [128, 2, 1], F32))
        negmr = e(nc.sbuf_tensor("negmr", [128, 2, 1], F32))
        probs_tm = e(nc.sbuf_tensor("probs_tm", [128, NT, NE], F32))
        mask_tm = e(nc.sbuf_tensor("mask_tm", [128, NT, NE], F32))
        tau_rep = e(nc.sbuf_tensor("tau_rep", [128, NE], F32))
        gw_bf = e(nc.sbuf_tensor("gw_bf", [128, 8, 8], BF16))
        eselsb = e(nc.sbuf_tensor("eselsb", [128, NE], F32))
        gmat = e(nc.sbuf_tensor("gmat_sb", [128, 128], F32))
        i8sb = e(nc.sbuf_tensor("i8sb", [NE, NE], F32))
        ones8 = e(nc.sbuf_tensor("ones8sb", [NE, 128], F32))
        t8sb = e(nc.sbuf_tensor("t8sb", [NE, 64], F32))
        rowm8 = e(nc.sbuf_tensor("rowm8sb", [64, 1], F32))
        sAB = ExitStack()
        eab = sAB.enter_context
        xt = eab(nc.sbuf_tensor("xt", [128, 17, D], F32))
        cb1rep = eab(nc.sbuf_tensor("cb1rep", [128, D], F32))
        sbrep = eab(nc.sbuf_tensor("sbrep", [128, D], F32))
        sR2 = ExitStack()
        er2 = sR2.enter_context
        hT = er2(nc.sbuf_tensor("hT", [128, 8, 2051], BF16))
        lnw = er2(nc.sbuf_tensor("lnw", [128, 1, D], F32))
        consts = er2(nc.sbuf_tensor("constsb", [1, 8], F32))
        crep = er2(nc.sbuf_tensor("crep", [128, 8], F32))
        onehot = er2(nc.sbuf_tensor("onehotb", [NCH, 8], F32))
        t16sb = er2(nc.sbuf_tensor("t16sbuf", [NCH, 128], F32))
        rowm = er2(nc.sbuf_tensor("rowmsb", [128, 1], F32))
        ccin_sb = er2(nc.sbuf_tensor("ccin_sb", [128, 8], F32))
        silc_d = er2(nc.sbuf_tensor("silc_d", [128, 8], F32))
        binrow = er2(nc.sbuf_tensor("binrow", [1, 80], F32))
        ones1 = er2(nc.sbuf_tensor("ones1", [1, 128], F32))
        garep = er2(nc.sbuf_tensor("garep", [128, D], F32))
        ca1d = er2(nc.sbuf_tensor("ca1d", [128, 8], F32))
        sad = er2(nc.sbuf_tensor("sad", [128, 8], F32))
        cbd = er2(nc.sbuf_tensor("cbd", [128, 8], F32))
        convw = er2(nc.sbuf_tensor("convwb", [128, 8, 4], F32))
        cwca = er2(nc.sbuf_tensor("cwca", [128, 8, 4], F32))
        wsum = er2(nc.sbuf_tensor("wsum", [128, 8], F32))
        sadw = er2(nc.sbuf_tensor("sadw", [128, 8], F32))
        sadw0 = er2(nc.sbuf_tensor("sadw0", [128, 8], F32))
        sadw3 = er2(nc.sbuf_tensor("sadw3", [128, 8], F32))
        wineff = er2(nc.sbuf_tensor("wineff", [128, 8, 80], F32))
        wink = er2(nc.sbuf_tensor("wink", [128, 4, 8, 80], BF16))
        beffr = er2(nc.sbuf_tensor("beffr", [1, 80], F32))
        beffc = er2(nc.sbuf_tensor("beffc", [80, 1], F32))
        btot = er2(nc.sbuf_tensor("btot", [80, 1], F32))
        bmix = er2(nc.sbuf_tensor("bmix", [80, 1], F32))
        e0c = er2(nc.sbuf_tensor("e0c", [80, 1], F32))
        e3c = er2(nc.sbuf_tensor("e3c", [80, 1], F32))
        ones11 = er2(nc.sbuf_tensor("ones11", [1, 1], F32))
        alogc = er2(nc.sbuf_tensor("alogc", [NCH, 1], F32))
        absA = er2(nc.sbuf_tensor("absA", [NCH, 1], F32))
        Ac = er2(nc.sbuf_tensor("Ac", [NCH, 1], F32))
        Acl = er2(nc.sbuf_tensor("Acl", [NCH, 1], F32))
        recA = er2(nc.sbuf_tensor("recA", [NCH, 1], F32))
        selA = er2(nc.sbuf_tensor("selA", [NCH, 1], F32))
        summ = er2(nc.sbuf_tensor("summ", [NCH, 8], F32))
        Ae = er2(nc.sbuf_tensor("Ae", [NCH, 8], F32))
        Het = er2(nc.sbuf_tensor("Het", [NCH, 8], F32))
        Cc = er2(nc.sbuf_tensor("Cc", [NCH, 8], F32))
        carry = er2(nc.sbuf_tensor("carry", [NCH, 1], F32))

        psAB = ExitStack()
        ps_t = psAB.enter_context(nc.psum_tensor("ps_t", [128, 2, 512], F32))
        ps_rep = psAB.enter_context(nc.psum_tensor("ps_rep", [128, 512], F32))
        ps_projS = ExitStack()
        ps_proj = ps_projS.enter_context(nc.psum_tensor("ps_proj", [80, 2, 512], F32))
        ps_stack = ExitStack()
        ps_mod = ps_stack.enter_context(nc.psum_tensor("ps_mod", [1, 2, 512], F32))
        ps_48 = ps_stack.enter_context(nc.psum_tensor("ps_48", [80, 512], F32))

        sc1 = ExitStack()
        cvec = sc1.enter_context(nc.sbuf_tensor("cvecb", [1, D], F32))
        silc = sc1.enter_context(nc.sbuf_tensor("silc", [1, D], F32))
        wmod_t = sc1.enter_context(nc.sbuf_tensor("wmod_t", [128, 8, 512], F32))
        modrow = sc1.enter_context(nc.sbuf_tensor("modrow", [1, 6 * D], F32))
        ga_row = sc1.enter_context(nc.sbuf_tensor("ga_row", [1, D], F32))
        gb_row = sc1.enter_context(nc.sbuf_tensor("gb_row", [1, D], F32))
        tmpr = sc1.enter_context(nc.sbuf_tensor("tmpr", [1, D], F32))
        tmpr2 = sc1.enter_context(nc.sbuf_tensor("tmpr2", [1, D], F32))
        ca1row = sc1.enter_context(nc.sbuf_tensor("ca1row", [1, D], F32))
        winb = sc1.enter_context(nc.sbuf_tensor("winb", [128, 8, 80], F32))

        P = Prog(nc, ctx)
        blk = e(nc.Block())

        def ld(name, out, in_, after=(), slow=False):
            P.op("sync", (lambda out=out, in_=in_, slow=slow: lambda h: h.dma_start(
                 out=out, in_=in_, allow_slow_non_contiguous=slow))(),
                 after=after, label=name, dma=True)
        ld("ld_x", xt[:], d_xt.rearrange("(m p) d -> p m d", p=128))
        ld("ld_cw", convw[:], d_convw.rearrange("(k p) w -> p k w", p=128))
        ld("ld_cbd", cbd[:], d_convb.rearrange("(k p) o -> p (k o)", p=128), slow=True)
        ld("ld_win", winb[:], d_win.rearrange("(k p) w -> p k w", p=128))
        ld("ld_binrow", binrow[:], d_bin[:])
        ld("ld_al", alogc[:], d_alog[:])
        ld("ld_cv", cvec[:], d_cvec[:])
        ld("ld_cst", consts[:], d_consts[:])
        ld("ld_oh", onehot[:], d_onehot[:])
        ld("ld_t16", t16sb[:], d_t16[:])
        ld("ld_rowm", rowm[:], d_rowm[:])
        ld("ld_esel", eselsb[:], d_esel[:])
        ld("ld_gmat", gmat[:], d_gmat[:])
        ld("ld_i8", i8sb[:], d_i8[:])
        ld("ld_ones8", ones8[:], d_ones8[:])
        ld("ld_t8", t8sb[:], d_t8[:])
        ld("ld_rowm8", rowm8[:], d_rowm8[:])
        P.op("gpsimd", lambda h: h.dma_start(out=gw_bf[:],
             in_=d_gatew.rearrange("(k p) w -> p k w", p=128)),
             label="ld_gw", dma=True)


        P.op("gpsimd", lambda h: h.memset(ident[:], 0.0))
        P.op("gpsimd", lambda h: h.affine_select(out=ident[:], in_=ident[:],
             compare_op=ALU.not_equal, fill=1.0, base=0, pattern=[[-1, 128]],
             channel_multiplier=1))
        P.op("gpsimd", lambda h: h.memset(ones1[:], 1.0))
        P.op("gpsimd", lambda h: h.memset(ones11[:], 1.0))
        P.op("gpsimd", lambda h: (h.memset(cap_t[:, 0:1], 1e-5),
             h.memset(cap_t[:, 1:2], 1.0), h.memset(cap_t[:, 2:3], -1.0))[-1],
             label="g_const")
        nc.const_aps.register(F32, 1e-5, cap_t[:, 0:1])
        nc.const_aps.register(F32, 1.0, cap_t[:, 1:2])
        nc.const_aps.register(F32, -1.0, cap_t[:, 2:3])

        P.op("scalar", lambda h: h.activation(silc[:], cvec[:], ACT.Silu),
             after=["ld_cv"], label="silu_c")
        ld("st_silc", d_rowscr[0:1, :], silc[:], after=["silu_c"])
        ld("ld_silcd", silc_d[:], d_rowscr[0:1, :].rearrange("o (k p) -> (o p) k", p=128),
           after=["st_silc"], slow=True)
        for ch in range(12):
            pp = ch % 2
            after = [] if ch < 1 else [f"mod_mm{ch - 1}"]
            ld(f"ld_wm{ch}", wmod_t[:],
               d_wmod.rearrange("(k p) c -> p k c", p=128)[:, :, 512 * ch:512 * (ch + 1)],
               after=after)

            def mmfn(h, ch=ch, pp=pp):
                last = None
                for dt_ in range(8):
                    last = h.matmul(ps_mod[:, pp, :], silc_d[:, dt_:dt_ + 1],
                                    wmod_t[:, dt_, :],
                                    start=(dt_ == 0), stop=(dt_ == 7))
                return last
            am = [f"ld_wm{ch}", "ld_silcd"] + ([f"mod_ev{ch - 2}"] if ch >= 2 else [])
            P.op("tensor", mmfn, after=am, label=f"mod_mm{ch}")
            P.op("scalar", (lambda ch=ch, pp=pp: lambda h: h.activation(
                modrow[:, 512 * ch:512 * (ch + 1)], ps_mod[:, pp, :], ACT.Identity))(),
                after=[f"mod_mm{ch}"], label=f"mod_ev{ch}")
        P.op("scalar", lambda h: h.nop(), after=[], label="mod_done")

        P.op("scalar", lambda h: (h.activation(ga_row[:], modrow[:, 2 * D:3 * D], ACT.Abs),
             h.activation(gb_row[:], modrow[:, 5 * D:6 * D], ACT.Abs))[-1],
             after=["mod_ev5", "mod_ev11"], label="gagb_abs")

        def gagb(h):
            last = None
            for (row, off) in ((ga_row, 2 * D), (gb_row, 5 * D)):
                src = modrow[:, off:off + D]
                h.tensor_scalar(tmpr[:], src, 0.0, None, ALU.is_ge)
                h.tensor_scalar(tmpr[:], tmpr[:], 2.0, -1.0, ALU.mult, ALU.add)
                h.tensor_scalar(tmpr2[:], row[:], 0.25, None, ALU.max)
                last = h.tensor_mul(row[:], tmpr[:], tmpr2[:])
            return last
        P.op("vector", gagb, after=["gagb_abs"], label="gagb")
        P.op("vector", lambda h: h.tensor_scalar(ca1row[:], modrow[:, 1 * D:2 * D],
             1.0, None, ALU.add), after=["mod_ev2", "mod_ev3"], label="ca1row")

        repspec = [("cb1a", modrow[:, 4 * D:4 * D + 512], cb1rep[:, 0:512], 1.0, ["mod_ev8"]),
                   ("cb1b", modrow[:, 4 * D + 512:5 * D], cb1rep[:, 512:1024], 1.0, ["mod_ev9"]),
                   ("sba", modrow[:, 3 * D:3 * D + 512], sbrep[:, 0:512], None, ["mod_ev6"]),
                   ("sbb", modrow[:, 3 * D + 512:4 * D], sbrep[:, 512:1024], None, ["mod_ev7"]),
                   ("gaa", ga_row[:, 0:512], garep[:, 0:512], None, ["gagb"]),
                   ("gab", ga_row[:, 512:1024], garep[:, 512:1024], None, ["gagb"]),
                   ("gba", gb_row[:, 0:512], gbrep[:, 0:512], None, ["gagb"]),
                   ("gbb", gb_row[:, 512:1024], gbrep[:, 512:1024], None, ["gagb"])]
        prevcopy = []
        for (nm, src, dst, post, dep) in repspec:
            P.op("tensor", (lambda src=src: lambda h: h.matmul(
                ps_rep[:, :], ones1[:, :], src, start=True, stop=True))(),
                after=dep + ["g_const"] + prevcopy, label=f"rp_{nm}")
            if post is not None:
                fn = (lambda dst=dst, post=post: lambda h: h.tensor_scalar(
                    dst, ps_rep[:, :], post, None, ALU.add))()
            else:
                fn = (lambda dst=dst: lambda h: h.tensor_copy(dst, ps_rep[:, :]))()
            P.op("vector", fn, after=[f"rp_{nm}"], label=f"rpc_{nm}")
            prevcopy = [f"rpc_{nm}"]
        P.op("tensor", lambda h: h.matmul(ps_rep[:, 0:8], ones1[:, :], consts[:, :],
             start=True, stop=True), after=["ld_cst"] + prevcopy, label="rp_cst")
        P.op("vector", lambda h: h.tensor_copy(crep[:], ps_rep[:, 0:8]),
             after=["rp_cst"], label="crep")

        ld("st_ca1", d_rowscr[1:2, :], ca1row[:], after=["ca1row"])
        ld("ld_ca1d", ca1d[:], d_rowscr[1:2, :].rearrange("o (k p) -> (o p) k", p=128),
           after=["st_ca1"], slow=True)
        ld("st_sa", d_rowscr[2:3, :], modrow[:, 0:D], after=["mod_ev0", "mod_ev1"])
        ld("ld_sad", sad[:], d_rowscr[2:3, :].rearrange("o (k p) -> (o p) k", p=128),
           after=["st_sa"], slow=True)

        if dbg is True:
            P.op("sync", lambda h: h.dma_start(out=dbg_out["mod"], in_=modrow[:]),
                 after=["mod_ev11", "gagb", "ca1row"], label="dbgmod", dma=True)

        for m in range(17):
            pp = m % 2
            def lnst(h, m=m):
                h.bn_stats(stats[:, 0:6], xt[:, m, 0:512])
                h.bn_stats(stats[:, 6:12], xt[:, m, 512:1024])
                h.bn_aggr(mv[:], stats[:, 0:6])
                h.bn_aggr(mv2[:], stats[:, 6:12])
                h.tensor_sub(mvd[:], mv[:, 0:1], mv2[:, 0:1])
                h.tensor_add(mv[:], mv[:], mv2[:])
                h.tensor_scalar(mv[:], mv[:], 0.5, None, ALU.mult)
                h.tensor_scalar(mvd[:], mvd[:], 0.5, None, ALU.mult)
                h.tensor_mul(mvd[:], mvd[:], mvd[:])
                return h.tensor_add(mv[:, 1:2], mv[:, 1:2], mvd[:])
            stafter = ["ld_x"] if m == 0 else [f"ln_nm{m - 1}"]
            if dbg is True and m == 1:
                stafter.append("dbgmv0")
            if dbg is True and m == 2:
                stafter.append("dbgrn0")
            P.op("vector", lnst, after=stafter, label=f"ln_st{m}")
            P.op("scalar", (lambda: lambda h: h.activation(stdt[:], mv[:, 1:2],
                 ACT.Sqrt, bias=1e-5))(), after=[f"ln_st{m}", "g_const"], label=f"ln_sq{m}")
            nmafter = [f"ln_sq{m}"]
            if dbg is True and m == 1:
                nmafter.append("dbgmv0")
            P.op("vector", (lambda pp=pp: lambda h: (h.reciprocal(rstd[:, pp, :], stdt[:]),
                 h.tensor_scalar(negmr[:, pp, :], mv[:, 0:1], rstd[:, pp, :], -1.0,
                                 ALU.mult, ALU.mult))[-1])(),
                 after=nmafter, label=f"ln_nm{m}")
            if dbg is True and m == 0:
                P.op("sync", lambda h: h.dma_start(out=dbg_out["mv0"], in_=mv[:]),
                     after=["ln_nm0"], label="dbgmv0", dma=True)
                P.op("sync", lambda h: h.dma_start(out=dbg_out["rn0"][:, 0:1], in_=rstd[:, 0, :],
                     allow_slow_non_contiguous=True),
                     after=["ln_nm0"], label="dbgrn0a", dma=True)
                P.op("sync", lambda h: h.dma_start(out=dbg_out["rn0"][:, 1:2], in_=negmr[:, 0, :],
                     allow_slow_non_contiguous=True),
                     after=["ln_nm0"], label="dbgrn0", dma=True)
            if dbg is True and m == 0:
                pass
            lnafter = [f"ln_nm{m}"]
            if m >= 1:
                lnafter.append(f"tp{m - 1}_7")
            if dbg is True and m == 1:
                lnafter.append("dbglnw0")
            P.op("scalar", (lambda m=m, pp=pp: lambda h: h.activation(lnw[:, 0, :],
                 xt[:, m, :], ACT.Identity, bias=negmr[:, pp, :], scale=rstd[:, pp, :]))(),
                 after=lnafter, label=f"ln{m}")
            if dbg is True and m == 0:
                P.op("sync", lambda h: h.dma_start(out=dbg_out["lnw0"], in_=lnw[:, 0, :]),
                     after=["ln0"], label="dbglnw0", dma=True)
            for k in range(8):
                gidx = m * 8 + k
                tppp = gidx % 2
                tpafter = [f"ln{m}", "g_const"]
                if gidx >= 2:
                    prev = gidx - 2
                    tpafter.append(f"tpc{prev // 8}_{prev % 8}")
                P.op("tensor", (lambda k=k, tppp=tppp: lambda h: h.transpose(
                    ps_t[:, tppp, 0:128], lnw[:, 0, 128 * k:128 * (k + 1)], ident[:]))(),
                    after=tpafter, label=f"tp{m}_{k}")
                if m < 16:
                    P.op("vector", (lambda m=m, k=k, tppp=tppp: lambda h: h.tensor_copy(
                        hT[:, k, 1 + 128 * m: 1 + 128 * (m + 1)], ps_t[:, tppp, 0:128]))(),
                        after=[f"tp{m}_{k}"], label=f"tpc{m}_{k}")
                else:
                    def cpy16(h, k=k, tppp=tppp):
                        h.tensor_copy(hT[:, k, 0:1], ps_t[:, tppp, 0:1])
                        return h.tensor_copy(hT[:, k, 2049:2051], ps_t[:, tppp, 1:3])
                    P.op("vector", cpy16, after=[f"tp{16}_{k}"], label=f"tpc{16}_{k}")
        last_tpc = "tpc16_7"

        def wineff_build(h):
            h.memset(wineff[:], 0.0)
            h.tensor_copy(wineff[:, :, 0:16], winb[:, :, 0:16])
            h.tensor_add(wineff[:, :, 32:48], winb[:, :, 16:48:2], winb[:, :, 17:48:2])
            h.tensor_add(wineff[:, :, 64:80], winb[:, :, 48:80:2], winb[:, :, 49:80:2])
            h.tensor_scalar(wineff[:, :, 32:48], wineff[:, :, 32:48], 0.5, None, ALU.mult)
            return h.tensor_scalar(wineff[:, :, 64:80], wineff[:, :, 64:80], 0.5,
                                   None, ALU.mult)
        P.op("vector", wineff_build, after=["ld_win"], label="wineff")

        def cwca_build(h):
            for dt_ in range(8):
                h.tensor_scalar(cwca[:, dt_, :], convw[:, dt_, :], ca1d[:, dt_:dt_ + 1],
                                None, ALU.mult)
            h.reduce_sum(wsum[:], convw[:], AX.X)
            for dt_ in range(8):
                h.tensor_scalar(sadw[:, dt_:dt_ + 1], wsum[:, dt_:dt_ + 1],
                                sad[:, dt_:dt_ + 1], None, ALU.mult)
                h.tensor_scalar(sadw0[:, dt_:dt_ + 1], convw[:, dt_, 0:1],
                                sad[:, dt_:dt_ + 1], None, ALU.mult)
                h.tensor_scalar(sadw3[:, dt_:dt_ + 1], convw[:, dt_, 3:4],
                                sad[:, dt_:dt_ + 1], None, ALU.mult)
            last = None
            for kk in range(4):
                for dt_ in range(8):
                    last = h.tensor_scalar(wink[:, kk, dt_, :], wineff[:, dt_, :],
                                           cwca[:, dt_, kk:kk + 1], None, ALU.mult)
            return last
        P.op("vector", cwca_build, after=["ld_cw", "ld_ca1d", "ld_sad", "wineff"],
             label="wink")

        def beff_build(h):
            h.memset(beffr[:], 0.0)
            h.tensor_copy(beffr[:, 0:16], binrow[:, 0:16])
            h.tensor_add(beffr[:, 32:48], binrow[:, 16:48:2], binrow[:, 17:48:2])
            h.tensor_add(beffr[:, 64:80], binrow[:, 48:80:2], binrow[:, 49:80:2])
            h.tensor_scalar(beffr[:, 32:48], beffr[:, 32:48], 0.5, None, ALU.mult)
            return h.tensor_scalar(beffr[:, 64:80], beffr[:, 64:80], 0.5, None, ALU.mult)
        P.op("vector", beff_build, after=["ld_binrow"], label="beffr")

        P.op("tensor", lambda h: h.matmul(ps_48[:, 0:1], beffr[:], ones11[:],
             start=True, stop=True), after=["beffr", "g_const"], label="beffc_mm")
        P.op("vector", lambda h: h.tensor_copy(beffc[:], ps_48[:, 0:1]),
             after=["beffc_mm"], label="beffc")

        def btot_mm(h):
            last = None
            for dt_ in range(8):
                h.matmul(ps_48[:, 1:2], wineff[:, dt_, :], cbd[:, dt_:dt_ + 1],
                         start=(dt_ == 0), stop=False)
            for dt_ in range(8):
                h.matmul(ps_48[:, 1:2], wineff[:, dt_, :], sadw[:, dt_:dt_ + 1],
                         start=False, stop=(dt_ == 7))
            for dt_ in range(8):
                last = h.matmul(ps_48[:, 2:3], wineff[:, dt_, :], sadw0[:, dt_:dt_ + 1],
                                start=(dt_ == 0), stop=(dt_ == 7))
            return last
        P.op("tensor", btot_mm, after=["wink", "ld_cbd", "beffc"], label="btot_mm")
        P.op("vector", lambda h: (h.tensor_add(btot[:], beffc[:], ps_48[:, 1:2]),
             h.tensor_copy(e0c[:], ps_48[:, 2:3]))[-1], after=["btot_mm"], label="btot")

        def e3_mm(h):
            last = None
            for dt_ in range(8):
                last = h.matmul(ps_48[:, 2:3], wineff[:, dt_, :], sadw3[:, dt_:dt_ + 1],
                                start=(dt_ == 0), stop=(dt_ == 7))
            return last
        P.op("tensor", e3_mm, after=["btot"], label="e3_mm")
        P.op("vector", lambda h: h.tensor_copy(e3c[:], ps_48[:, 2:3]),
             after=["e3_mm"], label="e3c")

        sc1.close()
        woutx = er2(nc.sbuf_tensor("woutx", [17, D], F32))
        ld("ld_wo", woutx[0:16, :], d_wout[:], after=["rpc_gbb", "wineff"])
        ld("ld_bo", woutx[16:17, :], d_bout[:], after=["rpc_gbb", "wineff"])
        projT = er2(nc.sbuf_tensor("projT", [80, TPC], F32))
        spt = er2(nc.sbuf_tensor("spt", [NCH, TPC], F32))
        dAt = er2(nc.sbuf_tensor("dAt", [NCH, TPC], F32))
        abar = er2(nc.sbuf_tensor("abar", [NCH, TPC], F32))
        bbar = er2(nc.sbuf_tensor("bbar", [NCH, TPC], F32))
        tay1 = spt
        bin_t = dAt
        Pt = er2(nc.sbuf_tensor("Pt", [NCH, TPC], F32))
        hloc = er2(nc.sbuf_tensor("hloc", [17, TPC], F32))
        c_t = er2(nc.sbuf_tensor("c_t", [NCH, TPC], F32))

        for nn in range(4):
            pp = nn % 2

            def projmm(h, nn=nn, pp=pp):
                last = None
                for kk in range(4):
                    for dt_ in range(8):
                        last = h.matmul(ps_proj[:, pp, :], wink[:, kk, dt_, :],
                                        hT[:, dt_, 512 * nn + kk: 512 * nn + kk + 512],
                                        start=(kk == 0 and dt_ == 0),
                                        stop=(kk == 3 and dt_ == 7))
                return last
            dep = ["wink", last_tpc]
            if nn >= 2:
                dep.append(f"proj_ev{nn - 2}")
            P.op("tensor", projmm, after=dep, label=f"proj_mm{nn}")
            evdep = [f"proj_mm{nn}", "btot"]
            if nn == 0:
                evdep += ["ld_ca1d", "ld_sad", "rpc_gbb"]
                if dbg is True:
                    evdep.append("dbgmod")
            P.op("scalar", (lambda nn=nn, pp=pp: lambda h: h.activation(
                projT[:, 512 * nn:512 * (nn + 1)], ps_proj[:, pp, :], ACT.Identity,
                bias=btot[:]))(), after=evdep, label=f"proj_ev{nn}")

        def projfix(h):
            h.tensor_scalar(bmix[:], e0c[:], crep[0:80, 0:1], None, ALU.mult)
            h.tensor_sub(projT[:, 0:1], projT[:, 0:1], bmix[:])
            h.tensor_scalar(bmix[:], e3c[:], crep[0:80, 1:2], None, ALU.mult)
            h.tensor_sub(projT[:, 2046:2047], projT[:, 2046:2047], bmix[:])
            h.tensor_scalar(projT[:, 2047:2048], projT[:, 2047:2048],
                            crep[0:80, 2:3], None, ALU.mult)
            h.tensor_scalar(bmix[:], beffc[:], crep[0:80, 2:3], None, ALU.mult)
            h.tensor_sub(bmix[:], beffc[:], bmix[:])
            return h.tensor_add(projT[:, 2047:2048], projT[:, 2047:2048], bmix[:])
        P.op("vector", projfix, after=["proj_ev0", "proj_ev1", "proj_ev2", "proj_ev3",
             "e3c", "crep"], label="projfix")

        P.op("scalar", lambda h: h.activation(absA[:], alogc[:], ACT.Exp),
             after=["ld_al"], label="absA")
        P.op("vector", lambda h: (h.tensor_scalar(Ac[:], absA[:], -1.0, None, ALU.mult),
             h.tensor_scalar_min(Acl[:], Ac[:], -1e-4),
             h.reciprocal(recA[:], Acl[:]),
             h.tensor_scalar(selA[:], absA[:], 1e-4, None, ALU.is_ge))[-1],
             after=["absA"], label="aprep")
        sp1dep = ["projfix", "ld_ca1d", "ld_sad", "rpc_gbb", "g_const"]
        if dbg is True:
            sp1dep.append("dbgmod")
        P.op("scalar", lambda h: (h.activation(spt[:], projT[0:16, :], ACT.Abs),
             h.activation(spt[:], spt[:], ACT.Exp, scale=-1.0),
             h.activation(spt[:], spt[:], ACT.Ln, bias=1.0))[-1],
             after=sp1dep, label="sp1")
        P.op("vector", lambda h: (h.tensor_relu(dAt[:], projT[0:16, :]),
             h.tensor_add(spt[:], spt[:], dAt[:]))[-1], after=["sp1"], label="sp2")
        P.op("vector", lambda h: h.tensor_scalar(dAt[:], spt[:], Ac[:], None, ALU.mult),
             after=["sp2", "aprep"], label="dA")
        P.op("scalar", lambda h: h.activation(abar[:], dAt[:], ACT.Exp),
             after=["dA"], label="abar")

        def bb1(h):
            h.tensor_scalar(bbar[:], abar[:], -1.0, recA[:], ALU.add, ALU.mult)
            h.tensor_scalar(tay1[:], dAt[:], 1.0 / 6.0, 0.5, ALU.mult, ALU.add)
            h.tensor_mul(tay1[:], tay1[:], dAt[:])
            h.tensor_scalar(tay1[:], tay1[:], 1.0, None, ALU.add)
            h.tensor_sub(bbar[:], bbar[:], tay1[:])
            h.tensor_scalar(bbar[:], bbar[:], selA[:], None, ALU.mult)
            return h.tensor_add(bbar[:], bbar[:], tay1[:])
        P.op("vector", bb1, after=["abar"], label="bb1")
        ld("st_bin", bin_t[:], projT[32:48, :], after=["projfix", "bb1", "dA"])
        ld("st_c", c_t[:], projT[64:80, :], after=["projfix"])
        P.op("vector", lambda h: h.tensor_mul(bbar[:], bbar[:], bin_t[:]),
             after=["bb1", "st_bin"], label="bbar")

        P.op("vector", lambda h: h.memset(hloc[:], 1.0), after=["abar"], label="hones")
        P.op("vector", lambda h: h.tensor_tensor_scan(Pt[:], abar[:], hloc[0:16, :], 1.0,
             ALU.mult, ALU.mult), after=["abar", "hones"], label="scanP")
        P.op("vector", lambda h: h.tensor_tensor_scan(hloc[0:16, :], abar[:], bbar[:], 0.0,
             ALU.mult, ALU.add), after=["bbar", "scanP"], label="scanH")

        P.op("vector", lambda h: (h.memset(summ[:], 0.0),
             h.tensor_copy(summ[:, 0:1], Pt[:, 2047:2048]),
             h.tensor_copy(summ[:, 1:2], hloc[0:16, 2047:2048]))[-1],
             after=["scanP", "scanH"], label="summ")
        P.op("tensor", lambda h: h.matmul(ps_rep[:, 0:8], t16sb[:], summ[:],
             start=True, stop=True), after=["summ", "ld_t16", "rp_cst"], label="summ_rep")
        P.op("vector", lambda h: h.tensor_scalar(ccin_sb[:], ps_rep[:, 0:8],
             rowm[:], None, ALU.mult), after=["summ_rep", "ld_rowm"], label="summ_msk")
        P.op("sync", lambda h: h.dma_start(out=d_ccin_s[:], in_=ccin_sb[:]),
             after=["summ_msk"], label="st_summ", dma=True)
        if dbg is True:
            P.op("sync", lambda h: h.dma_start(out=dbg_out["ccin"], in_=d_ccin_s[:]),
                 after=["st_summ"], label="dbgccin", dma=True)
        ccdep = ["st_summ"] + (["dbgccin"] if dbg is True else [])
        P.collective(lambda h: h.collective_compute(
            "AllReduce", ALU.add, replica_groups=[[0, 1, 2, 3, 4, 5, 6, 7]],
            ins=[d_ccin_s[:]], outs=[d_ccout_s[:]]), after=ccdep, label="cc_s")
        ld("ld_Ae", Ae[:], d_ccout_s.rearrange("(r c) t -> c r t", c=16)[:, :, 0],
           after=["cc_s"], slow=True)
        ld("ld_He", Het[:], d_ccout_s.rearrange("(r c) t -> c r t", c=16)[:, :, 1],
           after=["cc_s"], slow=True)

        def carry_fn(h):
            h.memset(Cc[:, 0:1], 0.0)
            h.memset(Cc[:, 4:5], 0.0)
            h.tensor_copy(Cc[:, 1:2], Het[:, 0:1])
            h.tensor_copy(Cc[:, 5:6], Het[:, 4:5])
            for j in (2, 3, 6, 7):
                h.tensor_mul(Cc[:, j:j + 1], Ae[:, j - 1:j], Cc[:, j - 1:j])
                h.tensor_add(Cc[:, j:j + 1], Cc[:, j:j + 1], Het[:, j - 1:j])
            h.tensor_mul(Cc[:], Cc[:], onehot[:])
            h.reduce_sum(carry[:], Cc[:], AX.X)
            return h.scalar_tensor_tensor(hloc[0:16, :], Pt[:], carry[:], hloc[0:16, :],
                                          ALU.mult, ALU.add)
        if dbg is True:
            P.op("sync", lambda h: h.dma_start(out=dbg_out["Ae"], in_=Ae[:]),
                 after=["ld_Ae"], label="dbgAe", dma=True)
            P.op("sync", lambda h: h.dma_start(out=dbg_out["He"], in_=Het[:]),
                 after=["ld_He"], label="dbgHe", dma=True)
            P.op("sync", lambda h: h.dma_start(out=dbg_out["summ"], in_=summ[:]),
                 after=["summ"], label="dbgsu", dma=True)
        if dbg is True:
            P.op("sync", lambda h: h.dma_start(out=dbg_out["ccraw"], in_=d_ccout_s[:]),
                 after=["cc_s"], label="dbgccr", dma=True)
        P.op("vector", carry_fn, after=["ld_Ae", "ld_He", "ld_oh"], label="hfix")
        if dbg is True:
            P.op("sync", lambda h: h.dma_start(out=dbg_out["carry"], in_=carry[:]),
                 after=["hfix"], label="dbgca", dma=True)

        chdep = ["hfix", "st_c"]
        if dbg is True:
            chdep.append("dbghsc")
        if dbg is True:
            P.op("sync", lambda h: h.dma_start(out=dbg_out["hsc"], in_=hloc[0:16, :]),
                 after=["hfix"], label="dbghsc", dma=True)
        P.op("vector", lambda h: h.tensor_mul(hloc[0:16, :], c_t[:], hloc[0:16, :]),
             after=chdep, label="CHx")
        P.op("vector", lambda h: h.tensor_mul(woutx[:], woutx[:], garep[0:17, :]),
             after=["ld_wo", "ld_bo", "rpc_gaa", "rpc_gab"], label="woutx")

        ps_stack.close()
        ps_yS = ExitStack()
        ps_y = ps_yS.enter_context(nc.psum_tensor("ps_y", [128, 2, 512], F32))

        for m in range(16):
            pdep = ["CHx", "woutx", "mod_done"]
            if m >= 1:
                pdep.append(f"x1_{m - 1}")

            def ymm(h, m=m):
                last = None
                for hh in range(2):
                    last = h.matmul(ps_y[:, hh, :], hloc[:, 128 * m:128 * (m + 1)],
                                    woutx[:, 512 * hh:512 * (hh + 1)],
                                    start=True, stop=True)
                return last
            P.op("tensor", ymm, after=pdep, label=f"ymm{m}")
            P.op("vector", (lambda m=m: lambda h: (h.tensor_add(
                xt[:, m, 0:512], xt[:, m, 0:512], ps_y[:, 0, :]),
                h.tensor_add(xt[:, m, 512:1024], xt[:, m, 512:1024], ps_y[:, 1, :]))[-1])(),
                after=[f"ymm{m}"], label=f"x1_{m}")
            P.op("sync", (lambda m=m: lambda h: h.dma_start(
                out=d_y[128 * m:128 * (m + 1), :], in_=xt[:, m, :]))(),
                after=[f"x1_{m}"], label=f"st_x1_{m}", dma=True)


        sR2.close()
        zrow_t = eab(nc.sbuf_tensor("zrow_t", [1, D], F32))
        hfw = eab(nc.sbuf_tensor("hfw", [128, D], F32))
        hfm = eab(nc.sbuf_tensor("hfm", [128, D], F32))
        hf_bf = eab(nc.sbuf_tensor("hf_bf", [128, D], BF16))
        hfT_t = eab(nc.sbuf_tensor("hfT_t", [128, 2, 8, 128], BF16))
        scoresT_t = eab(nc.sbuf_tensor("scoresT_t", [NE, 128], F32))
        nmax = eab(nc.sbuf_tensor("nmax", [128, 1], F32))
        prb = eab(nc.sbuf_tensor("prb", [128, 8], F32))
        rsum = eab(nc.sbuf_tensor("rsum", [128, 1], F32))
        probsT = eab(nc.sbuf_tensor("probsT", [NE, TPC], F32))
        ccin_p = eab(nc.sbuf_tensor("ccin_p_sb", [64, TPC], F32))
        P_bis = eab(nc.sbuf_tensor("P_bis", [128, 1024], F32))
        scr_b = eab(nc.sbuf_tensor("scr_b", [128, 1024], F32))
        lo_t = eab(nc.sbuf_tensor("lo_t", [128, 1], F32))
        hi_t = eab(nc.sbuf_tensor("hi_t", [128, 1], F32))
        tau_t = eab(nc.sbuf_tensor("tau_t", [128, 1], F32))
        sel_t = eab(nc.sbuf_tensor("sel_t", [128, 1], F32))
        cnt_t = eab(nc.sbuf_tensor("cnt_t", [128, 1], F32))
        bscr = eab(nc.sbuf_tensor("bscr", [128, 1], F32))
        tau8 = eab(nc.sbuf_tensor("tau8", [NE, 1], F32))
        d8 = eab(nc.sbuf_tensor("d8", [NE, NE], F32))

        ps_yS.close()
        ps_projS.close()
        ps_sc = psAB.enter_context(nc.psum_tensor("ps_sc", [NE, 2, 512], F32))
        ps_smx = psAB.enter_context(nc.psum_tensor("ps_smx", [128, 512], F32))
        ps_pp = psAB.enter_context(nc.psum_tensor("ps_pp", [64, 512], F32))

        P.op("vector", lambda h: h.memset(zrow_t[:], 0.0), label="zrow")
        P.op("sync", lambda h: h.dma_start(out=d_y[2048:2049, :], in_=zrow_t[0:1, :]),
             after=["zrow"], label="st_zrow", dma=True)
        P.op("gpsimd", lambda h: h.dma_start(out=d_hfsp[2048:2049, :], in_=zrow_t[0:1, :]),
             after=["zrow"], label="z_hf", dma=True)

        for m in range(16):
            pp = m % 2
            def ln2st(h, m=m):
                h.bn_stats(stats[:, 0:6], xt[:, m, 0:512])
                h.bn_stats(stats[:, 6:12], xt[:, m, 512:1024])
                h.bn_aggr(mv[:], stats[:, 0:6])
                h.bn_aggr(mv2[:], stats[:, 6:12])
                h.tensor_sub(mvd[:], mv[:, 0:1], mv2[:, 0:1])
                h.tensor_add(mv[:], mv[:], mv2[:])
                h.tensor_scalar(mv[:], mv[:], 0.5, None, ALU.mult)
                h.tensor_scalar(mvd[:], mvd[:], 0.5, None, ALU.mult)
                h.tensor_mul(mvd[:], mvd[:], mvd[:])
                return h.tensor_add(mv[:, 1:2], mv[:, 1:2], mvd[:])
            st2after = [f"x1_{m}"] if m == 0 else [f"x1_{m}", f"ln2_nm{m - 1}"]
            P.op("vector", ln2st, after=st2after, label=f"ln2_st{m}")
            P.op("scalar", (lambda: lambda h: h.activation(stdt[:], mv[:, 1:2],
                 ACT.Sqrt, bias=1e-5))(), after=[f"ln2_st{m}"], label=f"ln2_sq{m}")
            P.op("vector", (lambda pp=pp: lambda h: (h.reciprocal(rstd[:, pp, :], stdt[:]),
                 h.tensor_scalar(negmr[:, pp, :], mv[:, 0:1], rstd[:, pp, :], -1.0,
                                 ALU.mult, ALU.mult))[-1])(),
                 after=[f"ln2_sq{m}"], label=f"ln2_nm{m}")
            apafter = [f"ln2_nm{m}"]
            if m == 0:
                apafter += ["x1_15", "st_zrow"]
            else:
                apafter.append(f"hfmod{m - 1}")
            P.op("scalar", (lambda m=m, pp=pp: lambda h: h.activation(hfw[:],
                 xt[:, m, :], ACT.Identity, bias=negmr[:, pp, :], scale=rstd[:, pp, :]))(),
                 after=apafter, label=f"ln2ap{m}")
            hmafter = [f"ln2ap{m}"]
            if m >= 1:
                hmafter.append(f"st_hf{m - 1}")
                hmafter.append(f"tp2_{m - 1}_7")
            P.op("vector", (lambda: lambda h: (h.tensor_mul(hfm[:], hfw[:], cb1rep[:]),
                 h.tensor_add(hf_bf[:], hfm[:], sbrep[:]))[-1])(),
                 after=hmafter, label=f"hfmod{m}")
            P.op("sync", (lambda m=m: lambda h: h.dma_start(
                out=d_hfsp[128 * m:128 * (m + 1), :], in_=hf_bf[:]))(),
                after=[f"hfmod{m}", "z_hf"], label=f"st_hf{m}", dma=True)
            for k in range(8):
                gidx = m * 8 + k
                tppp = gidx % 2
                tpa = [f"hfmod{m}"]
                if gidx >= 2:
                    prv = gidx - 2
                    tpa.append(f"tpc2_{prv // 8}_{prv % 8}")
                P.op("tensor", (lambda k=k, tppp=tppp: lambda h: h.transpose(
                    ps_t[:, tppp, 0:128], hfw[:, 128 * k:128 * (k + 1)], ident[:]))(),
                    after=tpa, label=f"tp2_{m}_{k}")
                cpa = [f"tp2_{m}_{k}"]
                if m >= 2:
                    cpa.append(f"sc_mm{m - 2}")
                P.op("vector", (lambda m=m, k=k, tppp=tppp: lambda h: h.tensor_copy(
                    hfT_t[:, m % 2, k, :], ps_t[:, tppp, 0:128]))(),
                    after=cpa, label=f"tpc2_{m}_{k}")

            def scmm(h, m=m):
                last = None
                for dt_ in range(8):
                    last = h.matmul(ps_sc[:, m % 2, 0:128], gw_bf[:, dt_, :],
                                    hfT_t[:, m % 2, dt_, :],
                                    start=(dt_ == 0), stop=(dt_ == 7))
                return last
            sca = [f"tpc2_{m}_{k}" for k in range(8)] + ["ld_gw"]
            if m >= 2:
                sca.append(f"sc_ev{m - 2}")
            P.op("tensor", scmm, after=sca, label=f"sc_mm{m}")
            P.op("vector", (lambda m=m: lambda h: h.tensor_copy(
                scoresT_t[:], ps_sc[:, m % 2, 0:128]))(),
                after=[f"sc_mm{m}"] + ([f"sc_tp{m-1}"] if m >= 1 else []), label=f"sc_ev{m}")
            P.op("tensor", (lambda m=m: lambda h: h.transpose(
                ps_smx[:, 0:8], scoresT_t[:], ident[0:8, 0:8]))(),
                after=[f"sc_ev{m}"] + ([f"smx{m - 1}"] if m >= 1 else []),
                label=f"sc_tp{m}")

            P.op("vector", (lambda: lambda h: h.reduce_max(nmax[:], ps_smx[:, 0:8],
                 AX.X, negate=True))(), after=[f"sc_tp{m}"], label=f"smx_a{m}")
            P.op("scalar", (lambda m=m: lambda h: h.activation(prb[:], ps_smx[:, 0:8],
                 ACT.Exp, bias=nmax[:]))(), after=[f"smx_a{m}"], label=f"smx_e{m}")

            def smx2(h, m=m):
                h.reduce_sum(rsum[:], prb[:], AX.X)
                h.reciprocal(rsum[:], rsum[:])
                return h.tensor_scalar(probs_tm[:, m, :], prb[:], rsum[:], None, ALU.mult)
            P.op("vector", smx2, after=[f"smx_e{m}"], label=f"smx{m}")
            P.op("tensor", (lambda m=m: lambda h: h.transpose(
                ps_sc[:, m % 2, 128:256], probs_tm[:, m, :], ident[:]))(),
                after=[f"smx{m}"] + ([f"pr_cp{m - 1}"] if m >= 1 else []),
                label=f"pr_tp{m}")
            P.op("vector", (lambda m=m: lambda h: h.tensor_copy(
                probsT[:, 128 * m:128 * (m + 1)], ps_sc[:, m % 2, 128:256]))(),
                after=[f"pr_tp{m}"], label=f"pr_cp{m}")

        for chn in range(4):
            P.op("tensor", (lambda chn=chn: lambda h: h.matmul(ps_pp[:],
                 t8sb[:], probsT[:, 512 * chn:512 * (chn + 1)],
                 start=True, stop=True))(),
                 after=["pr_cp15", "ld_t8"] + ([f"pp_c{chn - 1}"] if chn else []),
                 label=f"pp_m{chn}")
            P.op("vector", (lambda chn=chn: lambda h: h.tensor_scalar(
                ccin_p[:, 512 * chn:512 * (chn + 1)], ps_pp[:], rowm8[:], None,
                ALU.mult))(), after=[f"pp_m{chn}", "ld_rowm8"], label=f"pp_c{chn}")
        P.op("sync", lambda h: h.dma_start(out=d_ccin_p[:], in_=ccin_p[:]),
             after=["pp_c3"], label="st_ccp", dma=True)
        P.collective(lambda h: h.collective_compute(
            "AllReduce", ALU.add, replica_groups=[[0, 1, 2, 3, 4, 5, 6, 7]],
            ins=[d_ccin_p[:]], outs=[d_ccout_p[:]]), after=["st_ccp"], label="cc_p")
        for bb2 in range(2):
            for s2 in range(4):
                for sp in range(2):
                    pb = bb2 * 64 + s2 * 16 + sp * 8
                    rr = (4 * bb2 + s2) * 8
                    ld(f"ld_pbis_{bb2}_{s2}_{sp}", P_bis[pb:pb + 8, :],
                       d_ccout_p[rr:rr + 8, 1024 * sp:1024 * (sp + 1)],
                       after=["cc_p"])
        P.op("sync", lambda h: h.dma_start(out=d_rowscr[0:1, 0:1], in_=d_rowscr[0:1, 0:1]),
             after=[f"ld_pbis_{b2}_{s2}_{sp}" for b2 in range(2) for s2 in range(4)
                    for sp in range(2)], label="ld_pbis", dma=True)

        P.op("vector", lambda h: (h.memset(lo_t[:], 0.0), h.memset(hi_t[:], 1.0),
             h.memset(tau_t[:], 0.5))[-1], after=["ld_pbis"], label="bis_init")
        for it in range(30):
            def bcnt(h):
                h.tensor_scalar(scr_b[:], P_bis[:], tau_t[:], None, ALU.is_ge)
                h.bn_stats(stats[:, 0:6], scr_b[:, 0:512])
                h.bn_stats(stats[:, 6:12], scr_b[:, 512:1024])
                h.bn_aggr(mv[:], stats[:, 0:6])
                h.bn_aggr(mv2[:], stats[:, 6:12])
                return h.tensor_scalar(cnt_t[:], mv[:, 0:1], mv2[:, 0:1], 512.0,
                                       ALU.add, ALU.mult)
            P.op("vector", bcnt,
                after=["bis_init"] if it == 0 else [f"bis_u{it - 1}"],
                label=f"bis_c{it}")
            P.op("tensor", (lambda: lambda h: h.matmul(ps_rep[:, 0:1], gmat[:],
                 cnt_t[:], start=True, stop=True))(),
                 after=[f"bis_c{it}", "ld_gmat"] + ([f"bis_u{it - 1}"] if it else []),
                 label=f"bis_m{it}")

            def bup(h, it=it):
                h.tensor_scalar(sel_t[:], ps_rep[:, 0:1], 2048.0, None, ALU.is_ge)
                h.tensor_sub(bscr[:], tau_t[:], lo_t[:])
                h.tensor_mul(bscr[:], bscr[:], sel_t[:])
                h.tensor_add(lo_t[:], lo_t[:], bscr[:])
                h.tensor_sub(bscr[:], hi_t[:], tau_t[:])
                h.tensor_mul(bscr[:], bscr[:], sel_t[:])
                h.tensor_add(hi_t[:], tau_t[:], bscr[:])
                h.tensor_add(tau_t[:], lo_t[:], hi_t[:])
                return h.tensor_scalar(tau_t[:], tau_t[:], 0.5, None, ALU.mult)
            P.op("vector", bup, after=[f"bis_m{it}"], label=f"bis_u{it}")

        P.op("tensor", lambda h: h.matmul(ps_rep[0:8, 4:5],
             eselsb[:], lo_t[:], start=True, stop=True),
             after=["bis_u29", "ld_esel"], label="tau_mm")
        P.op("vector", lambda h: h.tensor_copy(tau8[:], ps_rep[0:8, 4:5]),
             after=["tau_mm"], label="tau8")
        P.op("vector", lambda h: h.tensor_scalar(d8[:], i8sb[:], tau8[:], None, ALU.mult),
             after=["tau8", "ld_i8"], label="tau_d8")
        P.op("tensor", lambda h: h.matmul(ps_rep[:, 8:16], ones8[:], d8[:],
             start=True, stop=True), after=["tau_d8", "ld_ones8", "tau_mm"],
             label="tau_rp")
        P.op("vector", lambda h: h.tensor_copy(tau_rep[:], ps_rep[:, 8:16]),
             after=["tau_rp"], label="tau_rep")
        for m in range(16):
            P.op("vector", (lambda m=m: lambda h: h.tensor_tensor(
                mask_tm[:, m, :], probs_tm[:, m, :], tau_rep[:], ALU.is_ge))(),
                after=["tau_rep"], label=f"mask{m}")
        mask_all = [f"mask{m}" for m in range(16)]

        sAB.close()
        psAB.close()
        pose_all = e(nc.sbuf_tensor("pose_all", [128, NE, NT], F32))
        ps_gA = e(nc.psum_tensor("ps_gA", [128, 512], F32))
        ps_gB = e(nc.psum_tensor("ps_gB", [128, 128], F32))
        ps_mA = e(nc.psum_tensor("ps_mA", [128, 2, 512], F32))
        ps_mT = e(nc.psum_tensor("ps_mT", [128, 2, 512], F32))
        ps_y2 = e(nc.psum_tensor("ps_y2", [128, 2, 512], F32))

        sc = ExitStack()
        ec = sc.enter_context
        w2_bf = ec(nc.sbuf_tensor("w2_bf", [128, 32, D], BF16))
        h1 = ec(nc.sbuf_tensor("h1", [128, 32, CAP], BF16))
        hfbf = ec(nc.sbuf_tensor("hfbf", [128, NT, D], BF16))
        PT = ec(nc.sbuf_tensor("PT", [128, NT, CAP], BF16))
        xinT = ec(nc.sbuf_tensor("xinT", [128, 8, CAP], BF16))
        w1c = ec(nc.sbuf_tensor("w1c", [128, 2, 8, 128], BF16))
        iota640 = ec(nc.sbuf_tensor("iota640", [128, CAP], F32))
        probs_bf = ec(nc.sbuf_tensor("probs_bf", [128, NT, NE], BF16))
        p_sel = ec(nc.sbuf_tensor("p_sel", [128, 8], F32))
        ysel_jit = ec(nc.sbuf_tensor("ysel_jit", [128, 2, D], BF16))
        umat = ec(nc.sbuf_tensor("umat_sb", [128, 128], F32))
        mke = ec(nc.sbuf_tensor("mke", [128, NT], F32))
        zt16 = ec(nc.sbuf_tensor("zt16", [128, NT], F32))
        cse = ec(nc.sbuf_tensor("cse", [128, NT], F32))
        rowt = ec(nc.sbuf_tensor("rowt", [128, 8], F32))
        offs = ec(nc.sbuf_tensor("offs", [128, 8], F32))
        pose = ec(nc.sbuf_tensor("pose", [128, NT], F32))
        vald = ec(nc.sbuf_tensor("vald", [128, NT], F32))

        ld("ld_umat2", umat[:], d_umat[:], after=["mask15"])
        for ex in range(8):
            P.op("gpsimd", (lambda ex=ex: lambda h: h.dma_start(
                out=d_w2bf[HF * ex:HF * (ex + 1), :],
                in_=d_w2[HF * ex:HF * (ex + 1), :]))(),
                after=[f"pc2{ex - 1}"] if ex else [], label=f"pc2{ex}", dma=True)
        P.op("gpsimd", lambda h: h.dma_start(
            out=hfbf[:], in_=d_hfsp[0:2048, :].rearrange("(m p) d -> p m d", p=128)),
            after=["st_hf15", "st_ccp"], label="hfld", dma=True)
        P.op("gpsimd", lambda h: h.iota(iota640[:], pattern=[[1, CAP]], base=0,
             channel_multiplier=0, allow_small_or_imprecise_dtypes=True),
             after=["mask15"], label="iota6")
        P.op("vector", lambda h: h.tensor_copy(probs_bf[:], probs_tm[:]),
             after=["smx15", "mask15"], label="prbf")
        P.op("vector", lambda h: h.memset(zt16[:], 0.0), after=["mask15"], label="zt16")

        for ex in range(8):
            def comp_fixed(h, ex=ex):
                h.tensor_copy(mke[:], mask_tm[:, :, ex])
                h.tensor_tensor_scan(cse[:], zt16[:], mke[:], 0.0, ALU.add, ALU.add)
                return h.tensor_copy(rowt[:, ex:ex + 1], cse[:, NT - 1:NT])
            P.op("vector", comp_fixed, after=["zt16", "mask15"], label=f"cmp{ex}")
        P.op("tensor", lambda h: h.matmul(ps_gB[:, 0:8], umat[:], rowt[:],
             start=True, stop=True),
             after=[f"cmp{ex}" for ex in range(8)] + ["ld_umat2", "tau_rp", "bis_m29"],
             label="offs_mm")
        P.op("vector", lambda h: h.tensor_copy(offs[:], ps_gB[:, 0:8]),
             after=["offs_mm"], label="offs")
        for ex in range(8):
            def posfn(h, ex=ex):
                h.tensor_copy(mke[:], mask_tm[:, :, ex])
                h.tensor_tensor_scan(cse[:], zt16[:], mke[:], 0.0, ALU.add, ALU.add)
                h.tensor_sub(cse[:], cse[:], mke[:])
                h.tensor_scalar(pose[:], cse[:], offs[:, ex:ex + 1], None, ALU.add)
                h.tensor_scalar(vald[:], pose[:], float(CAP), None, ALU.is_lt)
                h.tensor_mul(vald[:], vald[:], mke[:])
                h.tensor_scalar(pose[:], pose[:], 1.0, None, ALU.add)
                h.tensor_mul(pose[:], pose[:], vald[:])
                return h.tensor_scalar(pose_all[:, ex, :], pose[:], -1.0, None, ALU.add)
            P.op("vector", posfn, after=["offs"], label=f"pos{ex}")

        for ex in range(8):
            def ptb(h, ex=ex):
                last = None
                for m in range(NT):
                    last = h.tensor_scalar(PT[:, m, :], iota640[:],
                                           pose_all[:, ex, m:m + 1], None, ALU.is_equal)
                return last
            pta = [f"pos{ex}", "iota6"]
            if ex > 0:
                pta += [f"g{ex - 1}_3", f"psl{ex - 1}"]
            P.op("vector", ptb, after=pta, label=f"PT{ex}")

            for q in range(8):
                def gfn(h, q=q):
                    last = None
                    for m in range(NT):
                        h.matmul(ps_gA[:], hfbf[:, m, 128 * q:128 * (q + 1)],
                                 PT[:, m, 0:512], start=(m == 0), stop=(m == NT - 1))
                        last = h.matmul(ps_gB[:],
                                        hfbf[:, m, 128 * q:128 * (q + 1)],
                                        PT[:, m, 512:CAP],
                                        start=(m == 0), stop=(m == NT - 1))
                    return last
                ga = [f"PT{ex}", "hfld"]
                if q > 0:
                    ga.append(f"gev{ex}_{q - 1}")
                else:
                    ga.append("offs")
                    if ex > 0:
                        ga += [f"gev{ex - 1}_7", f"pslc{ex - 1}"]
                P.op("tensor", gfn, after=ga, label=f"g{ex}_{q}")
                def gev(h, q=q):
                    h.activation(xinT[:, q, 0:512], ps_gA[:], ACT.Identity)
                    return h.activation(xinT[:, q, 512:CAP], ps_gB[:], ACT.Identity)
                P.op("scalar", gev, after=[f"g{ex}_{q}"], label=f"gev{ex}_{q}")

            def pslf(h, ex=ex):
                last = None
                for c in range(NST):
                    for m in range(NT):
                        last = h.matmul(ps_gB[:, c:c + 1],
                                        PT[:, m, 128 * c:128 * (c + 1)],
                                        probs_bf[:, m, ex:ex + 1],
                                        start=(m == 0), stop=(m == NT - 1))
                return last
            psla = [f"PT{ex}", "prbf", f"g{ex}_7", f"gev{ex}_7"]
            if ex > 0:
                psla.append(f"pslc{ex - 1}")
            P.op("tensor", pslf, after=psla, label=f"psl{ex}")
            pca = [f"psl{ex}"]
            if ex > 0:
                pca += [f"ysc{ex - 1}_{NST - 1}_0", f"ysc{ex - 1}_{NST - 1}_1"]
            P.op("vector", lambda h: h.tensor_copy(p_sel[:, 0:NST], ps_gB[:, 0:NST]),
                 after=pca, label=f"pslc{ex}")

            if ex == 0:
                for ch in range(2):
                    P.op("gpsimd", (lambda ch=ch: lambda h: h.dma_start(
                        out=w1c[:, ch % 2, :, :],
                        in_=d_w1.rearrange("(ee k p) f -> ee p k f", ee=8, p=128)[0][:, :, 128 * ch:128 * (ch + 1)]))(),
                        after=["tau_rep"], label=f"w1l0_{ch}", dma=True)

            for ch in range(32):
                par = ch % 2

                def m1fn(h, ex=ex, ch=ch, par=par):
                    last = None
                    for dt_ in range(8):
                        h.matmul(ps_mA[:, par, :], w1c[:, par, dt_, :],
                                 xinT[:, dt_, 0:512],
                                 start=(dt_ == 0), stop=(dt_ == 7))
                        last = h.matmul(ps_mT[:, par, 0:128], w1c[:, par, dt_, :],
                                        xinT[:, dt_, 512:CAP],
                                        start=(dt_ == 0), stop=(dt_ == 7))
                    return last
                if ch < 31:
                    nw1, nw2_ = ex, ch + 1
                elif ex < 7:
                    nw1, nw2_ = ex + 1, 0
                else:
                    nw1, nw2_ = 7, 31
                mdep = [f"w1l{nw1}_{nw2_}"] + [f"gev{ex}_{qq}" for qq in range(8)]
                if ch >= 2:
                    mdep.append(f"gel{ex}_{ch - 2}")
                elif ex > 0:
                    mdep.append(f"gel{ex - 1}_{30 + ch}")
                P.op("tensor", m1fn, after=mdep, label=f"m1_{ex}_{ch}")
                P.op("scalar", (lambda ex=ex, ch=ch, par=par: lambda h: (
                    h.activation(h1[:, ch, 0:512], ps_mA[:, par, :], ACT.Gelu),
                    h.activation(h1[:, ch, 512:CAP], ps_mT[:, par, 0:128], ACT.Gelu))[-1])(),
                    after=[f"m1_{ex}_{ch}"], label=f"gel{ex}_{ch}")
                nex, nch = ex, ch + 2
                if nch >= 32:
                    nex, nch = ex + 1, nch - 32
                if nex < 8:
                    P.op("gpsimd", (lambda nex=nex, nch=nch: lambda h: h.dma_start(
                        out=w1c[:, nch % 2, :, :],
                        in_=d_w1.rearrange("(ee k p) f -> ee p k f", ee=8, p=128)[nex][:, :, 128 * nch:128 * (nch + 1)]))(),
                        after=[f"m1_{ex}_{ch}"], label=f"w1l{nex}_{nch}", dma=True)

            w2dep = ["tau_rep", f"pc2{ex}"] if ex == 0 else \
                [f"m2_{ex - 1}_{NST - 1}_1", f"pc2{ex}"]
            for qq in range(4):
                P.op("sync", (lambda ex=ex, qq=qq: lambda h: h.dma_start(
                    out=w2_bf[:, 8 * qq:8 * (qq + 1), :],
                    in_=d_w2bf.rearrange("(ee t p) d2 -> ee p t d2", ee=8, p=128)[ex][:, 8 * qq:8 * (qq + 1), :]))(),
                    after=w2dep, label=f"w2l{ex}_{qq}", dma=True)

            gel_all = [f"gel{ex}_{ch}" for ch in range(32)]
            for t in range(NST):
                for dch in range(2):
                    pp = (t * 2 + dch) % 2

                    def m2fn(h, ex=ex, t=t, dch=dch, pp=pp):
                        last = None
                        for ht32 in range(32):
                            last = h.matmul(ps_y2[:, pp, :],
                                            h1[:, ht32, 128 * t:128 * (t + 1)],
                                            w2_bf[:, ht32, 512 * dch:512 * (dch + 1)],
                                            start=(ht32 == 0), stop=(ht32 == 31))
                        return last
                    m2dep = gel_all + [f"w2l{ex}_{qq}" for qq in range(4)]
                    if t > 0 or dch > 0:
                        ptd = (t, dch - 1) if dch else (t - 1, 1)
                        m2dep.append(f"ysc{ex}_{ptd[0]}_{ptd[1]}")
                    elif ex > 0:
                        m2dep.append(f"ysc{ex - 1}_{NST - 1}_1")
                    P.op("tensor", m2fn, after=m2dep, label=f"m2_{ex}_{t}_{dch}")
                    ysca = [f"m2_{ex}_{t}_{dch}", f"pslc{ex}", "rpc_gbb"]
                    if t >= 1:
                        ysca.append(f"yst{ex}_{t - 1}")
                    elif ex > 0:
                        ysca.append(f"yst{ex - 1}_{NST - 1}")
                    P.op("vector", (lambda ex=ex, t=t, dch=dch, pp=pp: lambda h:
                         h.scalar_tensor_tensor(
                             ysel_jit[:, t % 2, 512 * dch:512 * (dch + 1)],
                             ps_y2[:, pp, :], p_sel[:, t:t + 1],
                             gbrep[:, 512 * dch:512 * (dch + 1)],
                             ALU.mult, ALU.mult))(),
                         after=ysca, label=f"ysc{ex}_{t}_{dch}")
                P.op("sync", (lambda ex=ex, t=t: lambda h: h.dma_start(
                    out=d_ysel[CAP * ex + 128 * t:CAP * ex + 128 * (t + 1), :],
                    in_=ysel_jit[:, t % 2, :]))(),
                    after=[f"ysc{ex}_{t}_0", f"ysc{ex}_{t}_1"], label=f"yst{ex}_{t}",
                    dma=True)

        sc.close()
        ysel_all = e(nc.sbuf_tensor("ysel_all", [128, NE, NST, D], BF16))
        y_sb = e(nc.sbuf_tensor("y_sb", [128, NT, D], F32))
        PTT = e(nc.sbuf_tensor("PTT", [128, NST, 1024], BF16))
        pose_bc = e(nc.sbuf_tensor("pose_bc", [128, TPC], F32))
        iota5 = e(nc.sbuf_tensor("iota5", [128, NST], F32))
        onesm = e(nc.sbuf_tensor("onesm", [128, 128], F32))
        pvd = e(nc.sbuf_tensor("pvd", [128, 128], F32))

        fence = ([f"m2_7_{NST - 1}_{dd}" for dd in range(2)]
                 + [f"ysc7_{NST - 1}_{dd}" for dd in range(2)]
                 + [f"yst7_{t}" for t in range(NST)]
                 + ["gel7_31", "pslc7", "gev7_3", "g7_3", "psl7", "PT7"])
        for ex2 in range(8):
            P.op("sync", (lambda ex2=ex2: lambda h: h.dma_start(
                out=ysel_all[:, ex2, :, :],
                in_=d_ysel[CAP * ex2:CAP * (ex2 + 1), :].rearrange(
                    "(c p) d -> p c d", p=128)))(),
                after=fence + [f"yst{ex2}_{t}" for t in range(NST)],
                label=f"ysA{ex2}", dma=True)
        for m in range(NT):
            P.op("sync", (lambda m=m: lambda h: h.dma_start(
                out=y_sb[:, m, :], in_=d_y[128 * m:128 * (m + 1), :]))(),
                after=fence + [f"st_x1_{m}", "ysA7"], label=f"yin{m}", dma=True)
        P.op("gpsimd", lambda h: h.iota(iota5[:], pattern=[[128, NST]], base=0,
             channel_multiplier=1, allow_small_or_imprecise_dtypes=True),
             after=fence, label="iota5")
        P.op("gpsimd", lambda h: h.memset(onesm[:], 1.0), after=fence, label="onesm")

        ysA_all = [f"ysA{ex2}" for ex2 in range(8)]
        for ex in range(8):
            for m in range(NT):
                pva = [f"pos{ex}", "onesm", "iota5"]
                if m >= 1:
                    pva.append(f"bcm{ex}_{m - 1}")
                elif ex > 0:
                    pva.append(f"bcm{ex - 1}_{NT - 1}")
                else:
                    pva += fence
                P.op("vector", (lambda ex=ex, m=m: lambda h: h.tensor_scalar(
                    pvd[:, :], ident[:], pose_all[:, ex, m:m + 1], None, ALU.mult))(),
                    after=pva, label=f"pv{ex}_{m}")
                P.op("tensor", (lambda m=m: lambda h: h.matmul(
                    ps_mT[:, m % 2, 0:128], onesm[:], pvd[:], start=True, stop=True))(),
                    after=[f"pv{ex}_{m}"] + ([f"bcc{ex}_{m - 2}"] if m >= 2 else []) +
                    ([f"bcc{ex - 1}_{14 + m}"] if ex > 0 and m < 2 else []),
                    label=f"bcm{ex}_{m}")
                P.op("vector", (lambda ex=ex, m=m: lambda h: h.tensor_copy(
                    pose_bc[:, 128 * m:128 * (m + 1)], ps_mT[:, m % 2, 0:128]))(),
                    after=[f"bcm{ex}_{m}"], label=f"bcc{ex}_{m}")
            for hh in range(2):
                def pttf(h, ex=ex, hh=hh):
                    last = None
                    for c in range(NST):
                        last = h.tensor_scalar(PTT[:, c, :],
                                               pose_bc[:, 1024 * hh:1024 * (hh + 1)],
                                               iota5[:, c:c + 1], None, ALU.is_equal)
                    return last
                ptta = [f"bcc{ex}_{m}" for m in range(8 * hh, 8 * (hh + 1))] + ["iota5"]
                if hh == 1:
                    ptta.append(f"sc{ex}_7")
                elif ex > 0:
                    ptta.append(f"sc{ex - 1}_15")
                P.op("vector", pttf, after=ptta, label=f"ptt{ex}_{hh}")
                for mi in range(8):
                    m = 8 * hh + mi
                    psA = ps_mA if m % 2 == 0 else ps_y2

                    def scf(h, ex=ex, m=m, mi=mi, psA=psA):
                        last = None
                        for dch in range(2):
                            for c in range(NST):
                                last = h.matmul(psA[:, dch, :],
                                                PTT[:, c, 128 * mi:128 * (mi + 1)],
                                                ysel_all[:, ex, c, 512 * dch:512 * (dch + 1)],
                                                start=(c == 0), stop=(c == NST - 1))
                        return last
                    sca = [f"ptt{ex}_{hh}", "yin15"]
                    if ex > 0:
                        sca.append(f"ya{ex - 1}_{m}")
                    if m >= 2:
                        sca.append(f"ya{ex}_{m - 2}")
                    elif ex > 0:
                        sca.append(f"ya{ex - 1}_{14 + m}")
                    P.op("tensor", scf, after=sca, label=f"sc{ex}_{m}")

                    def yaf(h, ex=ex, m=m, psA=psA):
                        h.tensor_add(y_sb[:, m, 0:512], y_sb[:, m, 0:512], psA[:, 0, :])
                        return h.tensor_add(y_sb[:, m, 512:1024], y_sb[:, m, 512:1024],
                                            psA[:, 1, :])
                    P.op("vector", yaf, after=[f"sc{ex}_{m}"], label=f"ya{ex}_{m}")

        yst2_all = []
        for m in range(NT):
            P.op("sync", (lambda m=m: lambda h: h.dma_start(
                out=d_y[128 * m:128 * (m + 1), :], in_=y_sb[:, m, :]))(),
                after=[f"ya7_{m}"], label=f"yst2_{m}", dma=True)
            yst2_all.append(f"yst2_{m}")

        final_waits = [f"st_x1_{m}" for m in range(16)] + ["st_zrow"]
        final_waits += yst2_all
        if dbg == "c":
            probes = [
                ("tau8", tau8[:], ["tau8"]),
                ("pbis", P_bis[:], ["bis_u29"]),
                ("cnt", cnt_t[:], ["bis_u29"]),
                ("probs", probs_tm[:].rearrange("p a b -> p (a b)"), ["mask15"]),
                ("mask", mask_tm[:].rearrange("p a b -> p (a b)"), ["mask15"]),
                ("pose", pose_all[:].rearrange("p a b -> p (a b)"), [f"pos{x}" for x in range(8)]),
                ("psel", p_sel[:], ["pslc7"]),
                ("xinT", xinT[:, 0, :], ["gev7_3"]),
                ("h1d", h1[:, 0, :], ["gel7_31"]),
                ("ysl", ysel_all[:, 0, 0, :], ["ysA0"]),
                ("ysb", y_sb[:, 0, :], ["ya7_0"]),
                ("hfb", hfbf[:].rearrange("p a b -> p (a b)"), ["hfld", "g7_3"]),
                ("ptd", PT[:, 0, :], ["PT7", "g7_3", "psl7"]),
            ]
            for i, (nm, src, dep) in enumerate(probes):
                eng = "gpsimd" if src.dtype != F32 else "sync"
                P.op(eng, (lambda nm=nm, src=src: lambda h: h.dma_start(
                    out=dbg_out[nm], in_=src))(), after=dep,
                    label=f"dbc{i}", dma=True)
                final_waits.append(f"dbc{i}")
        if dbg is True:
            dbgspec = [("projT", dbg_out["projT"], projT[:], ["projfix", "sp2", "bbar", "CHx"]),
                       ("abar", dbg_out["abar"], abar[:], ["bbar", "scanP", "scanH"]),
                       ("bbar", dbg_out["bbar"], bbar[:], ["scanH"]),
                       ("hT0", dbg_out["hT0"], hT[:, 0, :], [last_tpc, "proj_mm3"])]
            for i, (nm, dst, src, dep) in enumerate(dbgspec):
                eng = "gpsimd" if nm == "hT0" else "sync"
                P.op(eng, (lambda dst=dst, src=src: lambda h: h.dma_start(out=dst, in_=src))(),
                     after=dep, label=f"dbg{i}", dma=True)
                final_waits.append(f"dbg{i}")
            final_waits.append("dbgmod")
            final_waits.append("dbghsc")
            final_waits += ["dbgmv0", "dbgrn0a", "dbgrn0", "dbglnw0",
                            "dbgAe", "dbgHe", "dbgca", "dbgsu", "dbgccr", "dbgccin"]

        def fin(h):
            return None
        P.op("gpsimd", fin, after=final_waits)

        P.emit(blk)

    return nc, dbg_out


def stage_inputs(inp):
    x = np.ascontiguousarray(inp["L0_attn_x"], dtype=np.float32)
    maps = []
    for c in range(8):
        b, q = c // 4, c % 4
        lo = 2048 * q
        xm = np.zeros((17 * 128, D), np.float32)
        xm[:2048] = x[b, lo:lo + 2048]
        if lo - 1 >= 0:
            xm[16 * 128 + 0] = x[b, lo - 1]
        if lo + 2048 < 8192:
            xm[16 * 128 + 1] = x[b, lo + 2048]
        if lo + 2049 < 8192:
            xm[16 * 128 + 2] = x[b, lo + 2049]
        consts = np.zeros((1, 8), np.float32)
        consts[0, 0] = 1.0 if q == 0 else 0.0
        consts[0, 1] = 1.0 if q == 3 else 0.0
        consts[0, 2] = 0.0 if q == 3 else 1.0
        onehot = np.zeros((NCH, 8), np.float32)
        onehot[:, c] = 1.0
        t16 = np.zeros((NCH, 128), np.float32)
        for k in range(NCH):
            t16[k, k::16] = 1.0
        rowm = np.zeros((128, 1), np.float32)
        rowm[16 * c:16 * (c + 1)] = 1.0
        cctest = np.zeros((128, 8), np.float32)
        cctest[16 * c:16 * (c + 1)] = 100.0 * (c + 1) + np.arange(8)[None, :]
        t8 = np.zeros((NE, 64), np.float32)
        for k in range(NE):
            t8[k, k::8] = 1.0
        rowm8 = np.zeros((64, 1), np.float32)
        rowm8[8 * c:8 * (c + 1)] = 1.0
        i8 = np.eye(NE, dtype=np.float32)
        ones8 = np.ones((NE, 128), np.float32)
        esel = np.zeros((128, NE), np.float32)
        for m in range(NE):
            esel[64 * b + m, m] = 1.0
        umat = (np.arange(128)[:, None] < np.arange(128)[None, :]).astype(np.float32)
        pidx = np.arange(128)
        gb_ = pidx // 64
        ge_ = pidx % 8
        gmat = ((gb_[:, None] == gb_[None, :]) & (ge_[:, None] == ge_[None, :])).astype(np.float32)
        maps.append({
            "t8": t8,
            "rowm8": rowm8,
            "i8": i8,
            "ones8": ones8,
            "esel": esel,
            "umat": umat,
            "gmat": gmat,

            "t16": t16,
            "rowm": rowm,
            "cctest": cctest,
            "xt_in": xm,
            "convw": np.ascontiguousarray(inp["L0_attn_conv_w"], dtype=np.float32).reshape(D, 4),
            "convb": np.ascontiguousarray(inp["L0_attn_conv_b"], dtype=np.float32).reshape(D, 1),
            "win": np.ascontiguousarray(inp["L0_attn_Win"], dtype=np.float32),
            "bin": np.ascontiguousarray(inp["L0_attn_b_in"], dtype=np.float32).reshape(1, 80),
            "wout": np.ascontiguousarray(inp["L0_attn_Wout"], dtype=np.float32),
            "bout": np.ascontiguousarray(inp["L0_attn_b_out"], dtype=np.float32).reshape(1, D),
            "alog": np.ascontiguousarray(inp["L0_attn_A_log"], dtype=np.float32).reshape(NCH, 1),
            "gatew": np.ascontiguousarray(inp["L0_ffn_gate_w"], dtype=np.float32),
            "cvec": np.ascontiguousarray(inp["L0_c"][b], dtype=np.float32).reshape(1, D),
            "wmod": np.ascontiguousarray(inp["L0_wmod"], dtype=np.float32),
            "w1": np.ascontiguousarray(inp["L0_ffn_w1"], dtype=np.float32).reshape(NE * D, HF),
            "w2": np.ascontiguousarray(inp["L0_ffn_w2"], dtype=np.float32).reshape(NE * HF, D),
            "consts": consts,
            "onehot": onehot,
        })
    return maps


_CACHE = {}


def kernel(**inputs):
    if "nc" not in _CACHE:
        _CACHE["nc"] = build(dbg=False)[0]
    nc = _CACHE["nc"]
    maps = stage_inputs(inputs)
    res = run_bass_kernel_spmd(nc, maps, core_ids=list(range(8)))
    out = np.zeros((2, 8192, D), np.float32)
    for c in range(8):
        b, q = c // 4, c % 4
        out[b, 2048 * q:2048 * (q + 1)] = res.results[c]["y"][:2048]
    return out


# version 04 (final)
# speedup vs baseline: 1.0254x; 1.0254x over previous
import numpy as np
import concourse.bass as bass
import concourse.mybir as mybir
from concourse.bass_utils import run_bass_kernel_spmd
from concourse.bass_core_rust import IndirectOffsetOnAxis
from contextlib import ExitStack

F32 = mybir.dt.float32
BF16 = mybir.dt.bfloat16
U32 = mybir.dt.uint32
ALU = mybir.AluOpType
ACT = mybir.ActivationFunctionType
AX = mybir.AxisListType

D = 1024
NT = 16
TPC = 2048
NCH = 16
NE = 8
HF = 4096
CAP = 640
NST = CAP // 128

ENGS = ("tensor", "vector", "scalar", "gpsimd", "sync")


_NO_DRAIN = {"wait_ge", "drain", "nop", "dma_start", "then_inc", "sem_inc",
             "collective_compute", "indirect_dma_start", "memset"}


class DrainWrap:

    def __init__(self, h):
        self._h = h

    def __getattr__(self, name):
        v = getattr(self._h, name)
        if callable(v) and not name.startswith("_") and name not in _NO_DRAIN:
            def wrapped(*a, **k):
                self._h.drain()
                return v(*a, **k)
            return wrapped
        return v



class Prog:

    def __init__(self, nc, ctx):
        self.nc = nc
        self.segs = [{g: [] for g in ENGS}]
        self.colls = []
        self.sem = {g: ctx.enter_context(nc.semaphore("sem_" + g)) for g in ENGS}
        self.count = {g: 0 for g in ENGS}
        self.labels = {}

    def op(self, eng, fn, after=(), label=None, dma=False):
        waits = []
        for a in after:
            src, val, srcdma = self.labels[a]
            if src != eng or srcdma:
                waits.append((src, val))
        if label is not None:
            self.count[eng] += 16 if dma else 1
            self.labels[label] = (eng, self.count[eng], dma)
        self.segs[-1][eng].append(dict(fn=fn, waits=waits, label=label, dma=dma))

    def collective(self, fn, after=(), label=None):
        waits = []
        for a in after:
            src, val, srcdma = self.labels[a]
            if src != "gpsimd" or srcdma:
                waits.append((src, val))
        if label is not None:
            self.count["gpsimd"] += 1
            self.labels[label] = ("gpsimd", self.count["gpsimd"], False)
        self.colls.append(dict(fn=fn, waits=waits, label=label))
        self.segs.append({g: [] for g in ENGS})

    def emit(self, blk):
        engmap = {"tensor": blk.tensor, "vector": blk.vector, "scalar": blk.scalar,
                  "gpsimd": blk.gpsimd, "sync": blk.sync}
        hiwater = {g: {gg: 0 for gg in ENGS} for g in ENGS}

        def make(eng_ops, eng_name):
            def body(h):
                hh = DrainWrap(h) if eng_name in ("vector", "scalar") else h
                hi = hiwater[eng_name]
                for o in eng_ops:
                    for (src, val) in o["waits"]:
                        if val > hi[src]:
                            h.wait_ge(self.sem[src], val)
                            hi[src] = val
                    inst = o["fn"](hh)
                    if o["label"] is not None:
                        assert inst is not None, f"label {o['label']} lacks inst"
                        if not o["dma"] and eng_name in ("vector", "scalar"):
                            inst = h.drain()
                        inst.then_inc(self.sem[eng_name], 16 if o["dma"] else 1)
            return body

        for si, seg in enumerate(self.segs):
            for eng in ENGS:
                if seg[eng]:
                    engmap[eng](make(seg[eng], eng))
            if si < len(self.colls):
                co = self.colls[si]

                def cbody(h, co=co):
                    hi = hiwater["gpsimd"]
                    for (src, val) in co["waits"]:
                        if val > hi[src]:
                            h.wait_ge(self.sem[src], val)
                            hi[src] = val
                    inst = co["fn"](h)
                    if co["label"] is not None:
                        inst.then_inc(self.sem["gpsimd"], 1)
                engmap["gpsimd"](cbody)


def build(dbg=False, stage="A"):
    nc = bass.Bass("TRN2", target_bir_lowering=False, debug=False)

    d_ccin_s = nc.dram_tensor("ccin_s", [128, 8], F32, kind="Internal").ap()
    d_ccout_s = nc.dram_tensor("ccout_s", [128, 8], F32, kind="Internal",
                               addr_space="Shared").ap()
    d_xt = nc.dram_tensor("xt_in", [17 * 128, D], F32, kind="ExternalInput").ap()
    d_convw = nc.dram_tensor("convw", [D, 4], F32, kind="ExternalInput").ap()
    d_convb = nc.dram_tensor("convb", [D, 1], F32, kind="ExternalInput").ap()
    d_win = nc.dram_tensor("win", [D, 80], F32, kind="ExternalInput").ap()
    d_bin = nc.dram_tensor("bin", [1, 80], F32, kind="ExternalInput").ap()
    d_wout = nc.dram_tensor("wout", [NCH, D], F32, kind="ExternalInput").ap()
    d_bout = nc.dram_tensor("bout", [1, D], F32, kind="ExternalInput").ap()
    d_alog = nc.dram_tensor("alog", [NCH, 1], F32, kind="ExternalInput").ap()
    d_gatew = nc.dram_tensor("gatew", [D, NE], F32, kind="ExternalInput").ap()
    d_cvec = nc.dram_tensor("cvec", [1, D], F32, kind="ExternalInput").ap()
    d_wmod = nc.dram_tensor("wmod", [D, 6 * D], F32, kind="ExternalInput").ap()
    d_w1 = nc.dram_tensor("w1", [NE * D, HF], F32, kind="ExternalInput").ap()
    d_w2 = nc.dram_tensor("w2", [NE * HF, D], F32, kind="ExternalInput").ap()
    d_consts = nc.dram_tensor("consts", [1, 8], F32, kind="ExternalInput").ap()
    d_onehot = nc.dram_tensor("onehot", [NCH, 8], F32, kind="ExternalInput").ap()
    d_t16 = nc.dram_tensor("t16", [NCH, 128], F32, kind="ExternalInput").ap()
    d_rowm = nc.dram_tensor("rowm", [128, 1], F32, kind="ExternalInput").ap()
    d_cctest = nc.dram_tensor("cctest", [128, 8], F32, kind="ExternalInput").ap()
    d_t8 = nc.dram_tensor("t8", [NE, 64], F32, kind="ExternalInput").ap()
    d_rowm8 = nc.dram_tensor("rowm8", [64, 1], F32, kind="ExternalInput").ap()
    d_i8 = nc.dram_tensor("i8", [NE, NE], F32, kind="ExternalInput").ap()
    d_ones8 = nc.dram_tensor("ones8", [NE, 128], F32, kind="ExternalInput").ap()
    d_esel = nc.dram_tensor("esel", [128, NE], F32, kind="ExternalInput").ap()
    d_umat = nc.dram_tensor("umat", [128, 128], F32, kind="ExternalInput").ap()
    d_gmat = nc.dram_tensor("gmat", [128, 128], F32, kind="ExternalInput").ap()

    d_y = nc.dram_tensor("y", [2049, D], F32, kind="ExternalOutput").ap()

    d_ccin_p = nc.dram_tensor("ccin_p", [8 * NE, TPC], F32, kind="Internal").ap()
    d_ccout_p = nc.dram_tensor("ccout_p", [8 * NE, TPC], F32, kind="Internal",
                               addr_space="Shared").ap()
    d_rowscr = nc.dram_tensor("rowscr", [3, D], F32, kind="Internal").ap()
    d_hfsp = nc.dram_tensor("hfsp", [2049, D], BF16, kind="Internal").ap()
    d_ysel = nc.dram_tensor("ysel", [NE * CAP, D], BF16, kind="Internal").ap()
    d_w2bf = nc.dram_tensor("w2bf", [NE * HF, D], BF16, kind="Internal").ap()

    dbg_out = {}
    if dbg == "c":
        def mkdbgc(name, shape, dt=F32):
            dbg_out[name] = nc.dram_tensor("dbg_" + name, shape, dt,
                                           kind="ExternalOutput").ap()
        mkdbgc("tau8", [NE, 1])
        mkdbgc("pbis", [128, 1024])
        mkdbgc("cnt", [128, 1])
        mkdbgc("probs", [128, NT * NE])
        mkdbgc("mask", [128, NT * NE])
        mkdbgc("pose", [128, NE * NT])
        mkdbgc("psel", [128, 8])
        mkdbgc("xinT", [128, CAP])
        mkdbgc("h1d", [128, CAP])
        mkdbgc("ysl", [128, D])
        mkdbgc("ysb", [128, D])
        mkdbgc("hfb", [128, NT * D])
        mkdbgc("ptd", [128, CAP])
    ctx = ExitStack()
    with ctx:
        e = ctx.enter_context
        if dbg is True:
            def mkdbg(name, shape):
                dbg_out[name] = nc.dram_tensor("dbg_" + name, shape, F32,
                                               kind="ExternalOutput").ap()
            mkdbg("projT", [80, TPC])
            mkdbg("abar", [NCH, TPC])
            mkdbg("bbar", [NCH, TPC])
            mkdbg("hsc", [NCH, TPC])
            mkdbg("mod", [1, 6 * D])
            mkdbg("hT0", [128, 2051])
            mkdbg("lnw0", [128, D])
            mkdbg("mv0", [128, 2])
            mkdbg("rn0", [128, 2])
            mkdbg("Ae", [NCH, 8])
            mkdbg("He", [NCH, 8])
            mkdbg("carry", [NCH, 1])
            mkdbg("summ", [NCH, 8])
            mkdbg("ccraw", [128, 8])
            mkdbg("ccin", [128, 8])

        ident = e(nc.sbuf_tensor("ident", [128, 128], F32))
        cap_t = e(nc.sbuf_tensor("cap_t", [128, 4], F32))
        gbrep = e(nc.sbuf_tensor("gbrep", [128, D], F32))
        stats = e(nc.sbuf_tensor("stats", [128, 12], F32))
        mv = e(nc.sbuf_tensor("mv", [128, 2], F32))
        mv2 = e(nc.sbuf_tensor("mv2", [128, 2], F32))
        mvd = e(nc.sbuf_tensor("mvd", [128, 1], F32))
        stdt = e(nc.sbuf_tensor("stdt", [128, 1], F32))
        rstd = e(nc.sbuf_tensor("rstd", [128, 2, 1], F32))
        negmr = e(nc.sbuf_tensor("negmr", [128, 2, 1], F32))
        probs_tm = e(nc.sbuf_tensor("probs_tm", [128, NT, NE], F32))
        mask_tm = e(nc.sbuf_tensor("mask_tm", [128, NT, NE], F32))
        tau_rep = e(nc.sbuf_tensor("tau_rep", [128, NE], F32))
        gw_bf = e(nc.sbuf_tensor("gw_bf", [128, 8, 8], BF16))
        eselsb = e(nc.sbuf_tensor("eselsb", [128, NE], F32))
        gmat = e(nc.sbuf_tensor("gmat_sb", [128, 128], F32))
        i8sb = e(nc.sbuf_tensor("i8sb", [NE, NE], F32))
        ones8 = e(nc.sbuf_tensor("ones8sb", [NE, 128], F32))
        t8sb = e(nc.sbuf_tensor("t8sb", [NE, 64], F32))
        rowm8 = e(nc.sbuf_tensor("rowm8sb", [64, 1], F32))
        sAB = ExitStack()
        eab = sAB.enter_context
        xt = eab(nc.sbuf_tensor("xt", [128, 17, D], F32))
        cb1rep = eab(nc.sbuf_tensor("cb1rep", [128, D], F32))
        sbrep = eab(nc.sbuf_tensor("sbrep", [128, D], F32))
        sR2 = ExitStack()
        er2 = sR2.enter_context
        hT = er2(nc.sbuf_tensor("hT", [128, 8, 2051], BF16))
        lnw = er2(nc.sbuf_tensor("lnw", [128, 1, D], F32))
        consts = er2(nc.sbuf_tensor("constsb", [1, 8], F32))
        crep = er2(nc.sbuf_tensor("crep", [128, 8], F32))
        onehot = er2(nc.sbuf_tensor("onehotb", [NCH, 8], F32))
        t16sb = er2(nc.sbuf_tensor("t16sbuf", [NCH, 128], F32))
        rowm = er2(nc.sbuf_tensor("rowmsb", [128, 1], F32))
        ccin_sb = er2(nc.sbuf_tensor("ccin_sb", [128, 8], F32))
        silc_d = er2(nc.sbuf_tensor("silc_d", [128, 8], F32))
        binrow = er2(nc.sbuf_tensor("binrow", [1, 80], F32))
        ones1 = er2(nc.sbuf_tensor("ones1", [1, 128], F32))
        garep = er2(nc.sbuf_tensor("garep", [128, D], F32))
        ca1d = er2(nc.sbuf_tensor("ca1d", [128, 8], F32))
        sad = er2(nc.sbuf_tensor("sad", [128, 8], F32))
        cbd = er2(nc.sbuf_tensor("cbd", [128, 8], F32))
        convw = er2(nc.sbuf_tensor("convwb", [128, 8, 4], F32))
        cwca = er2(nc.sbuf_tensor("cwca", [128, 8, 4], F32))
        wsum = er2(nc.sbuf_tensor("wsum", [128, 8], F32))
        sadw = er2(nc.sbuf_tensor("sadw", [128, 8], F32))
        sadw0 = er2(nc.sbuf_tensor("sadw0", [128, 8], F32))
        sadw3 = er2(nc.sbuf_tensor("sadw3", [128, 8], F32))
        wineff = er2(nc.sbuf_tensor("wineff", [128, 8, 80], F32))
        wink = er2(nc.sbuf_tensor("wink", [128, 4, 8, 80], BF16))
        beffr = er2(nc.sbuf_tensor("beffr", [1, 80], F32))
        beffc = er2(nc.sbuf_tensor("beffc", [80, 1], F32))
        btot = er2(nc.sbuf_tensor("btot", [80, 1], F32))
        bmix = er2(nc.sbuf_tensor("bmix", [80, 1], F32))
        e0c = er2(nc.sbuf_tensor("e0c", [80, 1], F32))
        e3c = er2(nc.sbuf_tensor("e3c", [80, 1], F32))
        ones11 = er2(nc.sbuf_tensor("ones11", [1, 1], F32))
        alogc = er2(nc.sbuf_tensor("alogc", [NCH, 1], F32))
        absA = er2(nc.sbuf_tensor("absA", [NCH, 1], F32))
        Ac = er2(nc.sbuf_tensor("Ac", [NCH, 1], F32))
        Acl = er2(nc.sbuf_tensor("Acl", [NCH, 1], F32))
        recA = er2(nc.sbuf_tensor("recA", [NCH, 1], F32))
        selA = er2(nc.sbuf_tensor("selA", [NCH, 1], F32))
        summ = er2(nc.sbuf_tensor("summ", [NCH, 8], F32))
        Ae = er2(nc.sbuf_tensor("Ae", [NCH, 8], F32))
        Het = er2(nc.sbuf_tensor("Het", [NCH, 8], F32))
        Cc = er2(nc.sbuf_tensor("Cc", [NCH, 8], F32))
        carry = er2(nc.sbuf_tensor("carry", [NCH, 1], F32))

        psAB = ExitStack()
        ps_t = psAB.enter_context(nc.psum_tensor("ps_t", [128, 2, 512], F32))
        ps_rep = psAB.enter_context(nc.psum_tensor("ps_rep", [128, 512], F32))
        ps_projS = ExitStack()
        ps_proj = ps_projS.enter_context(nc.psum_tensor("ps_proj", [80, 2, 512], F32))
        ps_stack = ExitStack()
        ps_mod = ps_stack.enter_context(nc.psum_tensor("ps_mod", [1, 2, 512], F32))
        ps_48 = ps_stack.enter_context(nc.psum_tensor("ps_48", [80, 512], F32))

        sc1 = ExitStack()
        cvec = sc1.enter_context(nc.sbuf_tensor("cvecb", [1, D], F32))
        silc = sc1.enter_context(nc.sbuf_tensor("silc", [1, D], F32))
        wmod_t = sc1.enter_context(nc.sbuf_tensor("wmod_t", [128, 8, 512], F32))
        modrow = sc1.enter_context(nc.sbuf_tensor("modrow", [1, 6 * D], F32))
        ga_row = sc1.enter_context(nc.sbuf_tensor("ga_row", [1, D], F32))
        gb_row = sc1.enter_context(nc.sbuf_tensor("gb_row", [1, D], F32))
        tmpr = sc1.enter_context(nc.sbuf_tensor("tmpr", [1, D], F32))
        tmpr2 = sc1.enter_context(nc.sbuf_tensor("tmpr2", [1, D], F32))
        ca1row = sc1.enter_context(nc.sbuf_tensor("ca1row", [1, D], F32))
        winb = sc1.enter_context(nc.sbuf_tensor("winb", [128, 8, 80], F32))

        P = Prog(nc, ctx)
        blk = e(nc.Block())

        def ld(name, out, in_, after=(), slow=False):
            P.op("sync", (lambda out=out, in_=in_, slow=slow: lambda h: h.dma_start(
                 out=out, in_=in_, allow_slow_non_contiguous=slow))(),
                 after=after, label=name, dma=True)
        ld("ld_x", xt[:], d_xt.rearrange("(m p) d -> p m d", p=128))
        ld("ld_cw", convw[:], d_convw.rearrange("(k p) w -> p k w", p=128))
        ld("ld_cbd", cbd[:], d_convb.rearrange("(k p) o -> p (k o)", p=128), slow=True)
        ld("ld_win", winb[:], d_win.rearrange("(k p) w -> p k w", p=128))
        ld("ld_binrow", binrow[:], d_bin[:])
        ld("ld_al", alogc[:], d_alog[:])
        ld("ld_cv", cvec[:], d_cvec[:])
        ld("ld_cst", consts[:], d_consts[:])
        ld("ld_oh", onehot[:], d_onehot[:])
        ld("ld_t16", t16sb[:], d_t16[:])
        ld("ld_rowm", rowm[:], d_rowm[:])
        ld("ld_esel", eselsb[:], d_esel[:])
        ld("ld_gmat", gmat[:], d_gmat[:])
        ld("ld_i8", i8sb[:], d_i8[:])
        ld("ld_ones8", ones8[:], d_ones8[:])
        ld("ld_t8", t8sb[:], d_t8[:])
        ld("ld_rowm8", rowm8[:], d_rowm8[:])
        P.op("gpsimd", lambda h: h.dma_start(out=gw_bf[:],
             in_=d_gatew.rearrange("(k p) w -> p k w", p=128)),
             label="ld_gw", dma=True)


        P.op("gpsimd", lambda h: h.memset(ident[:], 0.0))
        P.op("gpsimd", lambda h: h.affine_select(out=ident[:], in_=ident[:],
             compare_op=ALU.not_equal, fill=1.0, base=0, pattern=[[-1, 128]],
             channel_multiplier=1))
        P.op("gpsimd", lambda h: h.memset(ones1[:], 1.0))
        P.op("gpsimd", lambda h: h.memset(ones11[:], 1.0))
        P.op("gpsimd", lambda h: (h.memset(cap_t[:, 0:1], 1e-5),
             h.memset(cap_t[:, 1:2], 1.0), h.memset(cap_t[:, 2:3], -1.0))[-1],
             label="g_const")
        nc.const_aps.register(F32, 1e-5, cap_t[:, 0:1])
        nc.const_aps.register(F32, 1.0, cap_t[:, 1:2])
        nc.const_aps.register(F32, -1.0, cap_t[:, 2:3])

        P.op("scalar", lambda h: h.activation(silc[:], cvec[:], ACT.Silu),
             after=["ld_cv"], label="silu_c")
        ld("st_silc", d_rowscr[0:1, :], silc[:], after=["silu_c"])
        ld("ld_silcd", silc_d[:], d_rowscr[0:1, :].rearrange("o (k p) -> (o p) k", p=128),
           after=["st_silc"], slow=True)
        for ch in range(12):
            pp = ch % 2
            after = [] if ch < 1 else [f"mod_mm{ch - 1}"]
            ld(f"ld_wm{ch}", wmod_t[:],
               d_wmod.rearrange("(k p) c -> p k c", p=128)[:, :, 512 * ch:512 * (ch + 1)],
               after=after)

            def mmfn(h, ch=ch, pp=pp):
                last = None
                for dt_ in range(8):
                    last = h.matmul(ps_mod[:, pp, :], silc_d[:, dt_:dt_ + 1],
                                    wmod_t[:, dt_, :],
                                    start=(dt_ == 0), stop=(dt_ == 7))
                return last
            am = [f"ld_wm{ch}", "ld_silcd"] + ([f"mod_ev{ch - 2}"] if ch >= 2 else [])
            P.op("tensor", mmfn, after=am, label=f"mod_mm{ch}")
            P.op("scalar", (lambda ch=ch, pp=pp: lambda h: h.activation(
                modrow[:, 512 * ch:512 * (ch + 1)], ps_mod[:, pp, :], ACT.Identity))(),
                after=[f"mod_mm{ch}"], label=f"mod_ev{ch}")
        P.op("scalar", lambda h: h.nop(), after=[], label="mod_done")

        P.op("scalar", lambda h: (h.activation(ga_row[:], modrow[:, 2 * D:3 * D], ACT.Abs),
             h.activation(gb_row[:], modrow[:, 5 * D:6 * D], ACT.Abs))[-1],
             after=["mod_ev5", "mod_ev11"], label="gagb_abs")

        def gagb(h):
            last = None
            for (row, off) in ((ga_row, 2 * D), (gb_row, 5 * D)):
                src = modrow[:, off:off + D]
                h.tensor_scalar(tmpr[:], src, 0.0, None, ALU.is_ge)
                h.tensor_scalar(tmpr[:], tmpr[:], 2.0, -1.0, ALU.mult, ALU.add)
                h.tensor_scalar(tmpr2[:], row[:], 0.25, None, ALU.max)
                last = h.tensor_mul(row[:], tmpr[:], tmpr2[:])
            return last
        P.op("vector", gagb, after=["gagb_abs"], label="gagb")
        P.op("vector", lambda h: h.tensor_scalar(ca1row[:], modrow[:, 1 * D:2 * D],
             1.0, None, ALU.add), after=["mod_ev2", "mod_ev3"], label="ca1row")

        repspec = [("cb1a", modrow[:, 4 * D:4 * D + 512], cb1rep[:, 0:512], 1.0, ["mod_ev8"]),
                   ("cb1b", modrow[:, 4 * D + 512:5 * D], cb1rep[:, 512:1024], 1.0, ["mod_ev9"]),
                   ("sba", modrow[:, 3 * D:3 * D + 512], sbrep[:, 0:512], None, ["mod_ev6"]),
                   ("sbb", modrow[:, 3 * D + 512:4 * D], sbrep[:, 512:1024], None, ["mod_ev7"]),
                   ("gaa", ga_row[:, 0:512], garep[:, 0:512], None, ["gagb"]),
                   ("gab", ga_row[:, 512:1024], garep[:, 512:1024], None, ["gagb"]),
                   ("gba", gb_row[:, 0:512], gbrep[:, 0:512], None, ["gagb"]),
                   ("gbb", gb_row[:, 512:1024], gbrep[:, 512:1024], None, ["gagb"])]
        prevcopy = []
        for (nm, src, dst, post, dep) in repspec:
            P.op("tensor", (lambda src=src: lambda h: h.matmul(
                ps_rep[:, :], ones1[:, :], src, start=True, stop=True))(),
                after=dep + ["g_const"] + prevcopy, label=f"rp_{nm}")
            if post is not None:
                fn = (lambda dst=dst, post=post: lambda h: h.tensor_scalar(
                    dst, ps_rep[:, :], post, None, ALU.add))()
            else:
                fn = (lambda dst=dst: lambda h: h.tensor_copy(dst, ps_rep[:, :]))()
            P.op("vector", fn, after=[f"rp_{nm}"], label=f"rpc_{nm}")
            prevcopy = [f"rpc_{nm}"]
        P.op("tensor", lambda h: h.matmul(ps_rep[:, 0:8], ones1[:, :], consts[:, :],
             start=True, stop=True), after=["ld_cst"] + prevcopy, label="rp_cst")
        P.op("vector", lambda h: h.tensor_copy(crep[:], ps_rep[:, 0:8]),
             after=["rp_cst"], label="crep")

        ld("st_ca1", d_rowscr[1:2, :], ca1row[:], after=["ca1row"])
        ld("ld_ca1d", ca1d[:], d_rowscr[1:2, :].rearrange("o (k p) -> (o p) k", p=128),
           after=["st_ca1"], slow=True)
        ld("st_sa", d_rowscr[2:3, :], modrow[:, 0:D], after=["mod_ev0", "mod_ev1"])
        ld("ld_sad", sad[:], d_rowscr[2:3, :].rearrange("o (k p) -> (o p) k", p=128),
           after=["st_sa"], slow=True)

        if dbg is True:
            P.op("sync", lambda h: h.dma_start(out=dbg_out["mod"], in_=modrow[:]),
                 after=["mod_ev11", "gagb", "ca1row"], label="dbgmod", dma=True)

        for m in range(17):
            pp = m % 2
            def lnst(h, m=m):
                h.bn_stats(stats[:, 0:6], xt[:, m, 0:512])
                h.bn_stats(stats[:, 6:12], xt[:, m, 512:1024])
                h.bn_aggr(mv[:], stats[:, 0:6])
                h.bn_aggr(mv2[:], stats[:, 6:12])
                h.tensor_sub(mvd[:], mv[:, 0:1], mv2[:, 0:1])
                h.tensor_add(mv[:], mv[:], mv2[:])
                h.tensor_scalar(mv[:], mv[:], 0.5, None, ALU.mult)
                h.tensor_scalar(mvd[:], mvd[:], 0.5, None, ALU.mult)
                h.tensor_mul(mvd[:], mvd[:], mvd[:])
                return h.tensor_add(mv[:, 1:2], mv[:, 1:2], mvd[:])
            stafter = ["ld_x"] if m == 0 else [f"ln_nm{m - 1}"]
            if dbg is True and m == 1:
                stafter.append("dbgmv0")
            if dbg is True and m == 2:
                stafter.append("dbgrn0")
            P.op("vector", lnst, after=stafter, label=f"ln_st{m}")
            P.op("scalar", (lambda: lambda h: h.activation(stdt[:], mv[:, 1:2],
                 ACT.Sqrt, bias=1e-5))(), after=[f"ln_st{m}", "g_const"], label=f"ln_sq{m}")
            nmafter = [f"ln_sq{m}"]
            if dbg is True and m == 1:
                nmafter.append("dbgmv0")
            P.op("vector", (lambda pp=pp: lambda h: (h.reciprocal(rstd[:, pp, :], stdt[:]),
                 h.tensor_scalar(negmr[:, pp, :], mv[:, 0:1], rstd[:, pp, :], -1.0,
                                 ALU.mult, ALU.mult))[-1])(),
                 after=nmafter, label=f"ln_nm{m}")
            if dbg is True and m == 0:
                P.op("sync", lambda h: h.dma_start(out=dbg_out["mv0"], in_=mv[:]),
                     after=["ln_nm0"], label="dbgmv0", dma=True)
                P.op("sync", lambda h: h.dma_start(out=dbg_out["rn0"][:, 0:1], in_=rstd[:, 0, :],
                     allow_slow_non_contiguous=True),
                     after=["ln_nm0"], label="dbgrn0a", dma=True)
                P.op("sync", lambda h: h.dma_start(out=dbg_out["rn0"][:, 1:2], in_=negmr[:, 0, :],
                     allow_slow_non_contiguous=True),
                     after=["ln_nm0"], label="dbgrn0", dma=True)
            if dbg is True and m == 0:
                pass
            lnafter = [f"ln_nm{m}"]
            if m >= 1:
                lnafter.append(f"tp{m - 1}_7")
            if dbg is True and m == 1:
                lnafter.append("dbglnw0")
            P.op("scalar", (lambda m=m, pp=pp: lambda h: h.activation(lnw[:, 0, :],
                 xt[:, m, :], ACT.Identity, bias=negmr[:, pp, :], scale=rstd[:, pp, :]))(),
                 after=lnafter, label=f"ln{m}")
            if dbg is True and m == 0:
                P.op("sync", lambda h: h.dma_start(out=dbg_out["lnw0"], in_=lnw[:, 0, :]),
                     after=["ln0"], label="dbglnw0", dma=True)
            for k in range(8):
                gidx = m * 8 + k
                tppp = gidx % 2
                tpafter = [f"ln{m}", "g_const"]
                if gidx >= 2:
                    prev = gidx - 2
                    tpafter.append(f"tpc{prev // 8}_{prev % 8}")
                P.op("tensor", (lambda k=k, tppp=tppp: lambda h: h.transpose(
                    ps_t[:, tppp, 0:128], lnw[:, 0, 128 * k:128 * (k + 1)], ident[:]))(),
                    after=tpafter, label=f"tp{m}_{k}")
                if m < 16:
                    P.op("vector", (lambda m=m, k=k, tppp=tppp: lambda h: h.tensor_copy(
                        hT[:, k, 1 + 128 * m: 1 + 128 * (m + 1)], ps_t[:, tppp, 0:128]))(),
                        after=[f"tp{m}_{k}"], label=f"tpc{m}_{k}")
                else:
                    def cpy16(h, k=k, tppp=tppp):
                        h.tensor_copy(hT[:, k, 0:1], ps_t[:, tppp, 0:1])
                        return h.tensor_copy(hT[:, k, 2049:2051], ps_t[:, tppp, 1:3])
                    P.op("vector", cpy16, after=[f"tp{16}_{k}"], label=f"tpc{16}_{k}")
        last_tpc = "tpc16_7"

        def wineff_build(h):
            h.memset(wineff[:], 0.0)
            h.tensor_copy(wineff[:, :, 0:16], winb[:, :, 0:16])
            h.tensor_add(wineff[:, :, 32:48], winb[:, :, 16:48:2], winb[:, :, 17:48:2])
            h.tensor_add(wineff[:, :, 64:80], winb[:, :, 48:80:2], winb[:, :, 49:80:2])
            h.tensor_scalar(wineff[:, :, 32:48], wineff[:, :, 32:48], 0.5, None, ALU.mult)
            return h.tensor_scalar(wineff[:, :, 64:80], wineff[:, :, 64:80], 0.5,
                                   None, ALU.mult)
        P.op("vector", wineff_build, after=["ld_win"], label="wineff")

        def cwca_build(h):
            for dt_ in range(8):
                h.tensor_scalar(cwca[:, dt_, :], convw[:, dt_, :], ca1d[:, dt_:dt_ + 1],
                                None, ALU.mult)
            h.reduce_sum(wsum[:], convw[:], AX.X)
            for dt_ in range(8):
                h.tensor_scalar(sadw[:, dt_:dt_ + 1], wsum[:, dt_:dt_ + 1],
                                sad[:, dt_:dt_ + 1], None, ALU.mult)
                h.tensor_scalar(sadw0[:, dt_:dt_ + 1], convw[:, dt_, 0:1],
                                sad[:, dt_:dt_ + 1], None, ALU.mult)
                h.tensor_scalar(sadw3[:, dt_:dt_ + 1], convw[:, dt_, 3:4],
                                sad[:, dt_:dt_ + 1], None, ALU.mult)
            last = None
            for kk in range(4):
                for dt_ in range(8):
                    last = h.tensor_scalar(wink[:, kk, dt_, :], wineff[:, dt_, :],
                                           cwca[:, dt_, kk:kk + 1], None, ALU.mult)
            return last
        P.op("vector", cwca_build, after=["ld_cw", "ld_ca1d", "ld_sad", "wineff"],
             label="wink")

        def beff_build(h):
            h.memset(beffr[:], 0.0)
            h.tensor_copy(beffr[:, 0:16], binrow[:, 0:16])
            h.tensor_add(beffr[:, 32:48], binrow[:, 16:48:2], binrow[:, 17:48:2])
            h.tensor_add(beffr[:, 64:80], binrow[:, 48:80:2], binrow[:, 49:80:2])
            h.tensor_scalar(beffr[:, 32:48], beffr[:, 32:48], 0.5, None, ALU.mult)
            return h.tensor_scalar(beffr[:, 64:80], beffr[:, 64:80], 0.5, None, ALU.mult)
        P.op("vector", beff_build, after=["ld_binrow"], label="beffr")

        P.op("tensor", lambda h: h.matmul(ps_48[:, 0:1], beffr[:], ones11[:],
             start=True, stop=True), after=["beffr", "g_const"], label="beffc_mm")
        P.op("vector", lambda h: h.tensor_copy(beffc[:], ps_48[:, 0:1]),
             after=["beffc_mm"], label="beffc")

        def btot_mm(h):
            last = None
            for dt_ in range(8):
                h.matmul(ps_48[:, 1:2], wineff[:, dt_, :], cbd[:, dt_:dt_ + 1],
                         start=(dt_ == 0), stop=False)
            for dt_ in range(8):
                h.matmul(ps_48[:, 1:2], wineff[:, dt_, :], sadw[:, dt_:dt_ + 1],
                         start=False, stop=(dt_ == 7))
            for dt_ in range(8):
                last = h.matmul(ps_48[:, 2:3], wineff[:, dt_, :], sadw0[:, dt_:dt_ + 1],
                                start=(dt_ == 0), stop=(dt_ == 7))
            return last
        P.op("tensor", btot_mm, after=["wink", "ld_cbd", "beffc"], label="btot_mm")
        P.op("vector", lambda h: (h.tensor_add(btot[:], beffc[:], ps_48[:, 1:2]),
             h.tensor_copy(e0c[:], ps_48[:, 2:3]))[-1], after=["btot_mm"], label="btot")

        def e3_mm(h):
            last = None
            for dt_ in range(8):
                last = h.matmul(ps_48[:, 2:3], wineff[:, dt_, :], sadw3[:, dt_:dt_ + 1],
                                start=(dt_ == 0), stop=(dt_ == 7))
            return last
        P.op("tensor", e3_mm, after=["btot"], label="e3_mm")
        P.op("vector", lambda h: h.tensor_copy(e3c[:], ps_48[:, 2:3]),
             after=["e3_mm"], label="e3c")

        sc1.close()
        woutx = er2(nc.sbuf_tensor("woutx", [17, D], F32))
        ld("ld_wo", woutx[0:16, :], d_wout[:], after=["rpc_gbb", "wineff"])
        ld("ld_bo", woutx[16:17, :], d_bout[:], after=["rpc_gbb", "wineff"])
        projT = er2(nc.sbuf_tensor("projT", [80, TPC], F32))
        spt = er2(nc.sbuf_tensor("spt", [NCH, TPC], F32))
        dAt = er2(nc.sbuf_tensor("dAt", [NCH, TPC], F32))
        abar = er2(nc.sbuf_tensor("abar", [NCH, TPC], F32))
        bbar = er2(nc.sbuf_tensor("bbar", [NCH, TPC], F32))
        tay1 = spt
        bin_t = dAt
        Pt = er2(nc.sbuf_tensor("Pt", [NCH, TPC], F32))
        hloc = er2(nc.sbuf_tensor("hloc", [17, TPC], F32))
        c_t = er2(nc.sbuf_tensor("c_t", [NCH, TPC], F32))

        for nn in range(4):
            pp = nn % 2

            def projmm(h, nn=nn, pp=pp):
                last = None
                for kk in range(4):
                    for dt_ in range(8):
                        last = h.matmul(ps_proj[:, pp, :], wink[:, kk, dt_, :],
                                        hT[:, dt_, 512 * nn + kk: 512 * nn + kk + 512],
                                        start=(kk == 0 and dt_ == 0),
                                        stop=(kk == 3 and dt_ == 7))
                return last
            dep = ["wink", last_tpc]
            if nn >= 2:
                dep.append(f"proj_ev{nn - 2}")
            P.op("tensor", projmm, after=dep, label=f"proj_mm{nn}")
            evdep = [f"proj_mm{nn}", "btot"]
            if nn == 0:
                evdep += ["ld_ca1d", "ld_sad", "rpc_gbb"]
                if dbg is True:
                    evdep.append("dbgmod")
            P.op("scalar", (lambda nn=nn, pp=pp: lambda h: h.activation(
                projT[:, 512 * nn:512 * (nn + 1)], ps_proj[:, pp, :], ACT.Identity,
                bias=btot[:]))(), after=evdep, label=f"proj_ev{nn}")

        def projfix(h):
            h.tensor_scalar(bmix[:], e0c[:], crep[0:80, 0:1], None, ALU.mult)
            h.tensor_sub(projT[:, 0:1], projT[:, 0:1], bmix[:])
            h.tensor_scalar(bmix[:], e3c[:], crep[0:80, 1:2], None, ALU.mult)
            h.tensor_sub(projT[:, 2046:2047], projT[:, 2046:2047], bmix[:])
            h.tensor_scalar(projT[:, 2047:2048], projT[:, 2047:2048],
                            crep[0:80, 2:3], None, ALU.mult)
            h.tensor_scalar(bmix[:], beffc[:], crep[0:80, 2:3], None, ALU.mult)
            h.tensor_sub(bmix[:], beffc[:], bmix[:])
            return h.tensor_add(projT[:, 2047:2048], projT[:, 2047:2048], bmix[:])
        P.op("vector", projfix, after=["proj_ev0", "proj_ev1", "proj_ev2", "proj_ev3",
             "e3c", "crep"], label="projfix")

        P.op("scalar", lambda h: h.activation(absA[:], alogc[:], ACT.Exp),
             after=["ld_al"], label="absA")
        P.op("vector", lambda h: (h.tensor_scalar(Ac[:], absA[:], -1.0, None, ALU.mult),
             h.tensor_scalar_min(Acl[:], Ac[:], -1e-4),
             h.reciprocal(recA[:], Acl[:]),
             h.tensor_scalar(selA[:], absA[:], 1e-4, None, ALU.is_ge))[-1],
             after=["absA"], label="aprep")
        sp1dep = ["projfix", "ld_ca1d", "ld_sad", "rpc_gbb", "g_const"]
        if dbg is True:
            sp1dep.append("dbgmod")
        P.op("scalar", lambda h: (h.activation(spt[:], projT[0:16, :], ACT.Abs),
             h.activation(spt[:], spt[:], ACT.Exp, scale=-1.0),
             h.activation(spt[:], spt[:], ACT.Ln, bias=1.0))[-1],
             after=sp1dep, label="sp1")
        P.op("vector", lambda h: (h.tensor_relu(dAt[:], projT[0:16, :]),
             h.tensor_add(spt[:], spt[:], dAt[:]))[-1], after=["sp1"], label="sp2")
        P.op("vector", lambda h: h.tensor_scalar(dAt[:], spt[:], Ac[:], None, ALU.mult),
             after=["sp2", "aprep"], label="dA")
        P.op("scalar", lambda h: h.activation(abar[:], dAt[:], ACT.Exp),
             after=["dA"], label="abar")

        def bb1(h):
            h.tensor_scalar(bbar[:], abar[:], -1.0, recA[:], ALU.add, ALU.mult)
            h.tensor_scalar(tay1[:], dAt[:], 1.0 / 6.0, 0.5, ALU.mult, ALU.add)
            h.tensor_mul(tay1[:], tay1[:], dAt[:])
            h.tensor_scalar(tay1[:], tay1[:], 1.0, None, ALU.add)
            h.tensor_sub(bbar[:], bbar[:], tay1[:])
            h.tensor_scalar(bbar[:], bbar[:], selA[:], None, ALU.mult)
            return h.tensor_add(bbar[:], bbar[:], tay1[:])
        P.op("vector", bb1, after=["abar"], label="bb1")
        ld("st_bin", bin_t[:], projT[32:48, :], after=["projfix", "bb1", "dA"])
        ld("st_c", c_t[:], projT[64:80, :], after=["projfix"])
        P.op("vector", lambda h: h.tensor_mul(bbar[:], bbar[:], bin_t[:]),
             after=["bb1", "st_bin"], label="bbar")

        P.op("vector", lambda h: h.memset(hloc[:], 1.0), after=["abar"], label="hones")
        P.op("vector", lambda h: h.tensor_tensor_scan(Pt[:], abar[:], hloc[0:16, :], 1.0,
             ALU.mult, ALU.mult), after=["abar", "hones"], label="scanP")
        P.op("vector", lambda h: h.tensor_tensor_scan(hloc[0:16, :], abar[:], bbar[:], 0.0,
             ALU.mult, ALU.add), after=["bbar", "scanP"], label="scanH")

        P.op("vector", lambda h: (h.memset(summ[:], 0.0),
             h.tensor_copy(summ[:, 0:1], Pt[:, 2047:2048]),
             h.tensor_copy(summ[:, 1:2], hloc[0:16, 2047:2048]))[-1],
             after=["scanP", "scanH"], label="summ")
        P.op("tensor", lambda h: h.matmul(ps_rep[:, 0:8], t16sb[:], summ[:],
             start=True, stop=True), after=["summ", "ld_t16", "rp_cst"], label="summ_rep")
        P.op("vector", lambda h: h.tensor_scalar(ccin_sb[:], ps_rep[:, 0:8],
             rowm[:], None, ALU.mult), after=["summ_rep", "ld_rowm"], label="summ_msk")
        P.op("sync", lambda h: h.dma_start(out=d_ccin_s[:], in_=ccin_sb[:]),
             after=["summ_msk"], label="st_summ", dma=True)
        if dbg is True:
            P.op("sync", lambda h: h.dma_start(out=dbg_out["ccin"], in_=d_ccin_s[:]),
                 after=["st_summ"], label="dbgccin", dma=True)
        ccdep = ["st_summ"] + (["dbgccin"] if dbg is True else [])
        P.collective(lambda h: h.collective_compute(
            "AllReduce", ALU.add, replica_groups=[[0, 1, 2, 3, 4, 5, 6, 7]],
            ins=[d_ccin_s[:]], outs=[d_ccout_s[:]]), after=ccdep, label="cc_s")
        ld("ld_Ae", Ae[:], d_ccout_s.rearrange("(r c) t -> c r t", c=16)[:, :, 0],
           after=["cc_s"], slow=True)
        ld("ld_He", Het[:], d_ccout_s.rearrange("(r c) t -> c r t", c=16)[:, :, 1],
           after=["cc_s"], slow=True)

        def carry_fn(h):
            h.memset(Cc[:, 0:1], 0.0)
            h.memset(Cc[:, 4:5], 0.0)
            h.tensor_copy(Cc[:, 1:2], Het[:, 0:1])
            h.tensor_copy(Cc[:, 5:6], Het[:, 4:5])
            for j in (2, 3, 6, 7):
                h.tensor_mul(Cc[:, j:j + 1], Ae[:, j - 1:j], Cc[:, j - 1:j])
                h.tensor_add(Cc[:, j:j + 1], Cc[:, j:j + 1], Het[:, j - 1:j])
            h.tensor_mul(Cc[:], Cc[:], onehot[:])
            h.reduce_sum(carry[:], Cc[:], AX.X)
            return h.scalar_tensor_tensor(hloc[0:16, :], Pt[:], carry[:], hloc[0:16, :],
                                          ALU.mult, ALU.add)
        if dbg is True:
            P.op("sync", lambda h: h.dma_start(out=dbg_out["Ae"], in_=Ae[:]),
                 after=["ld_Ae"], label="dbgAe", dma=True)
            P.op("sync", lambda h: h.dma_start(out=dbg_out["He"], in_=Het[:]),
                 after=["ld_He"], label="dbgHe", dma=True)
            P.op("sync", lambda h: h.dma_start(out=dbg_out["summ"], in_=summ[:]),
                 after=["summ"], label="dbgsu", dma=True)
        if dbg is True:
            P.op("sync", lambda h: h.dma_start(out=dbg_out["ccraw"], in_=d_ccout_s[:]),
                 after=["cc_s"], label="dbgccr", dma=True)
        P.op("vector", carry_fn, after=["ld_Ae", "ld_He", "ld_oh"], label="hfix")
        if dbg is True:
            P.op("sync", lambda h: h.dma_start(out=dbg_out["carry"], in_=carry[:]),
                 after=["hfix"], label="dbgca", dma=True)

        chdep = ["hfix", "st_c"]
        if dbg is True:
            chdep.append("dbghsc")
        if dbg is True:
            P.op("sync", lambda h: h.dma_start(out=dbg_out["hsc"], in_=hloc[0:16, :]),
                 after=["hfix"], label="dbghsc", dma=True)
        P.op("vector", lambda h: h.tensor_mul(hloc[0:16, :], c_t[:], hloc[0:16, :]),
             after=chdep, label="CHx")
        P.op("vector", lambda h: h.tensor_mul(woutx[:], woutx[:], garep[0:17, :]),
             after=["ld_wo", "ld_bo", "rpc_gaa", "rpc_gab"], label="woutx")

        ps_stack.close()
        ps_yS = ExitStack()
        ps_y = ps_yS.enter_context(nc.psum_tensor("ps_y", [128, 2, 512], F32))

        for m in range(16):
            pdep = ["CHx", "woutx", "mod_done"]
            if m >= 1:
                pdep.append(f"x1_{m - 1}")

            def ymm(h, m=m):
                last = None
                for hh in range(2):
                    last = h.matmul(ps_y[:, hh, :], hloc[:, 128 * m:128 * (m + 1)],
                                    woutx[:, 512 * hh:512 * (hh + 1)],
                                    start=True, stop=True)
                return last
            P.op("tensor", ymm, after=pdep, label=f"ymm{m}")
            P.op("vector", (lambda m=m: lambda h: (h.tensor_add(
                xt[:, m, 0:512], xt[:, m, 0:512], ps_y[:, 0, :]),
                h.tensor_add(xt[:, m, 512:1024], xt[:, m, 512:1024], ps_y[:, 1, :]))[-1])(),
                after=[f"ymm{m}"], label=f"x1_{m}")
            P.op("sync", (lambda m=m: lambda h: h.dma_start(
                out=d_y[128 * m:128 * (m + 1), :], in_=xt[:, m, :]))(),
                after=[f"x1_{m}"], label=f"st_x1_{m}", dma=True)


        sR2.close()
        zrow_t = eab(nc.sbuf_tensor("zrow_t", [1, D], F32))
        hfw = eab(nc.sbuf_tensor("hfw", [128, D], F32))
        hfm = eab(nc.sbuf_tensor("hfm", [128, D], F32))
        hf_bf = eab(nc.sbuf_tensor("hf_bf", [128, D], BF16))
        hfT_t = eab(nc.sbuf_tensor("hfT_t", [128, 2, 8, 128], BF16))
        scoresT_t = eab(nc.sbuf_tensor("scoresT_t", [NE, 128], F32))
        nmax = eab(nc.sbuf_tensor("nmax", [128, 1], F32))
        prb = eab(nc.sbuf_tensor("prb", [128, 8], F32))
        rsum = eab(nc.sbuf_tensor("rsum", [128, 1], F32))
        probsT = eab(nc.sbuf_tensor("probsT", [NE, TPC], F32))
        ccin_p = eab(nc.sbuf_tensor("ccin_p_sb", [64, TPC], F32))
        P_bis = eab(nc.sbuf_tensor("P_bis", [128, 1024], F32))
        scr_b = eab(nc.sbuf_tensor("scr_b", [128, 1024], F32))
        lo_t = eab(nc.sbuf_tensor("lo_t", [128, 1], F32))
        hi_t = eab(nc.sbuf_tensor("hi_t", [128, 1], F32))
        tau_t = eab(nc.sbuf_tensor("tau_t", [128, 1], F32))
        sel_t = eab(nc.sbuf_tensor("sel_t", [128, 1], F32))
        cnt_t = eab(nc.sbuf_tensor("cnt_t", [128, 1], F32))
        bscr = eab(nc.sbuf_tensor("bscr", [128, 1], F32))
        tau8 = eab(nc.sbuf_tensor("tau8", [NE, 1], F32))
        d8 = eab(nc.sbuf_tensor("d8", [NE, NE], F32))

        ps_yS.close()
        ps_projS.close()
        ps_sc = psAB.enter_context(nc.psum_tensor("ps_sc", [NE, 2, 512], F32))
        ps_smx = psAB.enter_context(nc.psum_tensor("ps_smx", [128, 512], F32))
        ps_pp = psAB.enter_context(nc.psum_tensor("ps_pp", [64, 512], F32))

        P.op("vector", lambda h: h.memset(zrow_t[:], 0.0), label="zrow")
        P.op("sync", lambda h: h.dma_start(out=d_y[2048:2049, :], in_=zrow_t[0:1, :]),
             after=["zrow"], label="st_zrow", dma=True)
        P.op("gpsimd", lambda h: h.dma_start(out=d_hfsp[2048:2049, :], in_=zrow_t[0:1, :]),
             after=["zrow"], label="z_hf", dma=True)

        for m in range(16):
            pp = m % 2
            def ln2st(h, m=m):
                h.bn_stats(stats[:, 0:6], xt[:, m, 0:512])
                h.bn_stats(stats[:, 6:12], xt[:, m, 512:1024])
                h.bn_aggr(mv[:], stats[:, 0:6])
                h.bn_aggr(mv2[:], stats[:, 6:12])
                h.tensor_sub(mvd[:], mv[:, 0:1], mv2[:, 0:1])
                h.tensor_add(mv[:], mv[:], mv2[:])
                h.tensor_scalar(mv[:], mv[:], 0.5, None, ALU.mult)
                h.tensor_scalar(mvd[:], mvd[:], 0.5, None, ALU.mult)
                h.tensor_mul(mvd[:], mvd[:], mvd[:])
                return h.tensor_add(mv[:, 1:2], mv[:, 1:2], mvd[:])
            st2after = [f"x1_{m}"] if m == 0 else [f"x1_{m}", f"ln2_nm{m - 1}"]
            P.op("vector", ln2st, after=st2after, label=f"ln2_st{m}")
            P.op("scalar", (lambda: lambda h: h.activation(stdt[:], mv[:, 1:2],
                 ACT.Sqrt, bias=1e-5))(), after=[f"ln2_st{m}"], label=f"ln2_sq{m}")
            P.op("vector", (lambda pp=pp: lambda h: (h.reciprocal(rstd[:, pp, :], stdt[:]),
                 h.tensor_scalar(negmr[:, pp, :], mv[:, 0:1], rstd[:, pp, :], -1.0,
                                 ALU.mult, ALU.mult))[-1])(),
                 after=[f"ln2_sq{m}"], label=f"ln2_nm{m}")
            apafter = [f"ln2_nm{m}"]
            if m == 0:
                apafter += ["x1_15", "st_zrow"]
            else:
                apafter.append(f"hfmod{m - 1}")
            P.op("scalar", (lambda m=m, pp=pp: lambda h: h.activation(hfw[:],
                 xt[:, m, :], ACT.Identity, bias=negmr[:, pp, :], scale=rstd[:, pp, :]))(),
                 after=apafter, label=f"ln2ap{m}")
            hmafter = [f"ln2ap{m}"]
            if m >= 1:
                hmafter.append(f"st_hf{m - 1}")
                hmafter.append(f"tp2_{m - 1}_7")
            P.op("vector", (lambda: lambda h: (h.tensor_mul(hfm[:], hfw[:], cb1rep[:]),
                 h.tensor_add(hf_bf[:], hfm[:], sbrep[:]))[-1])(),
                 after=hmafter, label=f"hfmod{m}")
            P.op("sync", (lambda m=m: lambda h: h.dma_start(
                out=d_hfsp[128 * m:128 * (m + 1), :], in_=hf_bf[:]))(),
                after=[f"hfmod{m}", "z_hf"], label=f"st_hf{m}", dma=True)
            for k in range(8):
                gidx = m * 8 + k
                tppp = gidx % 2
                tpa = [f"hfmod{m}"]
                if gidx >= 2:
                    prv = gidx - 2
                    tpa.append(f"tpc2_{prv // 8}_{prv % 8}")
                P.op("tensor", (lambda k=k, tppp=tppp: lambda h: h.transpose(
                    ps_t[:, tppp, 0:128], hfw[:, 128 * k:128 * (k + 1)], ident[:]))(),
                    after=tpa, label=f"tp2_{m}_{k}")
                cpa = [f"tp2_{m}_{k}"]
                if m >= 2:
                    cpa.append(f"sc_mm{m - 2}")
                P.op("vector", (lambda m=m, k=k, tppp=tppp: lambda h: h.tensor_copy(
                    hfT_t[:, m % 2, k, :], ps_t[:, tppp, 0:128]))(),
                    after=cpa, label=f"tpc2_{m}_{k}")

            def scmm(h, m=m):
                last = None
                for dt_ in range(8):
                    last = h.matmul(ps_sc[:, m % 2, 0:128], gw_bf[:, dt_, :],
                                    hfT_t[:, m % 2, dt_, :],
                                    start=(dt_ == 0), stop=(dt_ == 7))
                return last
            sca = [f"tpc2_{m}_{k}" for k in range(8)] + ["ld_gw"]
            if m >= 2:
                sca.append(f"sc_ev{m - 2}")
            P.op("tensor", scmm, after=sca, label=f"sc_mm{m}")
            P.op("vector", (lambda m=m: lambda h: h.tensor_copy(
                scoresT_t[:], ps_sc[:, m % 2, 0:128]))(),
                after=[f"sc_mm{m}"] + ([f"sc_tp{m-1}"] if m >= 1 else []), label=f"sc_ev{m}")
            P.op("tensor", (lambda m=m: lambda h: h.transpose(
                ps_smx[:, 0:8], scoresT_t[:], ident[0:8, 0:8]))(),
                after=[f"sc_ev{m}"] + ([f"smx{m - 1}"] if m >= 1 else []),
                label=f"sc_tp{m}")

            P.op("vector", (lambda: lambda h: h.reduce_max(nmax[:], ps_smx[:, 0:8],
                 AX.X, negate=True))(), after=[f"sc_tp{m}"], label=f"smx_a{m}")
            P.op("scalar", (lambda m=m: lambda h: h.activation(prb[:], ps_smx[:, 0:8],
                 ACT.Exp, bias=nmax[:]))(), after=[f"smx_a{m}"], label=f"smx_e{m}")

            def smx2(h, m=m):
                h.reduce_sum(rsum[:], prb[:], AX.X)
                h.reciprocal(rsum[:], rsum[:])
                return h.tensor_scalar(probs_tm[:, m, :], prb[:], rsum[:], None, ALU.mult)
            P.op("vector", smx2, after=[f"smx_e{m}"], label=f"smx{m}")
            P.op("tensor", (lambda m=m: lambda h: h.transpose(
                ps_sc[:, m % 2, 128:256], probs_tm[:, m, :], ident[:]))(),
                after=[f"smx{m}"] + ([f"pr_cp{m - 1}"] if m >= 1 else []),
                label=f"pr_tp{m}")
            P.op("vector", (lambda m=m: lambda h: h.tensor_copy(
                probsT[:, 128 * m:128 * (m + 1)], ps_sc[:, m % 2, 128:256]))(),
                after=[f"pr_tp{m}"], label=f"pr_cp{m}")

        for chn in range(4):
            P.op("tensor", (lambda chn=chn: lambda h: h.matmul(ps_pp[:],
                 t8sb[:], probsT[:, 512 * chn:512 * (chn + 1)],
                 start=True, stop=True))(),
                 after=["pr_cp15", "ld_t8"] + ([f"pp_c{chn - 1}"] if chn else []),
                 label=f"pp_m{chn}")
            P.op("vector", (lambda chn=chn: lambda h: h.tensor_scalar(
                ccin_p[:, 512 * chn:512 * (chn + 1)], ps_pp[:], rowm8[:], None,
                ALU.mult))(), after=[f"pp_m{chn}", "ld_rowm8"], label=f"pp_c{chn}")
        P.op("sync", lambda h: h.dma_start(out=d_ccin_p[:], in_=ccin_p[:]),
             after=["pp_c3"], label="st_ccp", dma=True)
        P.collective(lambda h: h.collective_compute(
            "AllReduce", ALU.add, replica_groups=[[0, 1, 2, 3, 4, 5, 6, 7]],
            ins=[d_ccin_p[:]], outs=[d_ccout_p[:]]), after=["st_ccp"], label="cc_p")
        for bb2 in range(2):
            for s2 in range(4):
                for sp in range(2):
                    pb = bb2 * 64 + s2 * 16 + sp * 8
                    rr = (4 * bb2 + s2) * 8
                    ld(f"ld_pbis_{bb2}_{s2}_{sp}", P_bis[pb:pb + 8, :],
                       d_ccout_p[rr:rr + 8, 1024 * sp:1024 * (sp + 1)],
                       after=["cc_p"])
        P.op("sync", lambda h: h.dma_start(out=d_rowscr[0:1, 0:1], in_=d_rowscr[0:1, 0:1]),
             after=[f"ld_pbis_{b2}_{s2}_{sp}" for b2 in range(2) for s2 in range(4)
                    for sp in range(2)], label="ld_pbis", dma=True)

        P.op("vector", lambda h: (h.memset(lo_t[:], 0.0), h.memset(hi_t[:], 1.0),
             h.memset(tau_t[:], 0.5))[-1], after=["ld_pbis"], label="bis_init")
        for it in range(30):
            def bcnt(h):
                h.tensor_scalar(scr_b[:], P_bis[:], tau_t[:], None, ALU.is_ge)
                h.bn_stats(stats[:, 0:6], scr_b[:, 0:512])
                h.bn_stats(stats[:, 6:12], scr_b[:, 512:1024])
                h.bn_aggr(mv[:], stats[:, 0:6])
                h.bn_aggr(mv2[:], stats[:, 6:12])
                return h.tensor_scalar(cnt_t[:], mv[:, 0:1], mv2[:, 0:1], 512.0,
                                       ALU.add, ALU.mult)
            P.op("vector", bcnt,
                after=["bis_init"] if it == 0 else [f"bis_u{it - 1}"],
                label=f"bis_c{it}")
            P.op("tensor", (lambda: lambda h: h.matmul(ps_rep[:, 0:1], gmat[:],
                 cnt_t[:], start=True, stop=True))(),
                 after=[f"bis_c{it}", "ld_gmat"] + ([f"bis_u{it - 1}"] if it else []),
                 label=f"bis_m{it}")

            def bup(h, it=it):
                h.tensor_scalar(sel_t[:], ps_rep[:, 0:1], 2048.0, None, ALU.is_ge)
                h.tensor_sub(bscr[:], tau_t[:], lo_t[:])
                h.tensor_mul(bscr[:], bscr[:], sel_t[:])
                h.tensor_add(lo_t[:], lo_t[:], bscr[:])
                h.tensor_sub(bscr[:], hi_t[:], tau_t[:])
                h.tensor_mul(bscr[:], bscr[:], sel_t[:])
                h.tensor_add(hi_t[:], tau_t[:], bscr[:])
                h.tensor_add(tau_t[:], lo_t[:], hi_t[:])
                return h.tensor_scalar(tau_t[:], tau_t[:], 0.5, None, ALU.mult)
            P.op("vector", bup, after=[f"bis_m{it}"], label=f"bis_u{it}")

        P.op("tensor", lambda h: h.matmul(ps_rep[0:8, 4:5],
             eselsb[:], lo_t[:], start=True, stop=True),
             after=["bis_u29", "ld_esel"], label="tau_mm")
        P.op("vector", lambda h: h.tensor_copy(tau8[:], ps_rep[0:8, 4:5]),
             after=["tau_mm"], label="tau8")
        P.op("vector", lambda h: h.tensor_scalar(d8[:], i8sb[:], tau8[:], None, ALU.mult),
             after=["tau8", "ld_i8"], label="tau_d8")
        P.op("tensor", lambda h: h.matmul(ps_rep[:, 8:16], ones8[:], d8[:],
             start=True, stop=True), after=["tau_d8", "ld_ones8", "tau_mm"],
             label="tau_rp")
        P.op("vector", lambda h: h.tensor_copy(tau_rep[:], ps_rep[:, 8:16]),
             after=["tau_rp"], label="tau_rep")
        for m in range(16):
            P.op("vector", (lambda m=m: lambda h: h.tensor_tensor(
                mask_tm[:, m, :], probs_tm[:, m, :], tau_rep[:], ALU.is_ge))(),
                after=["tau_rep"], label=f"mask{m}")
        mask_all = [f"mask{m}" for m in range(16)]

        sAB.close()
        psAB.close()
        pose_all = e(nc.sbuf_tensor("pose_all", [128, NE, NT], F32))
        ps_gA = e(nc.psum_tensor("ps_gA", [128, 512], F32))
        ps_gB = e(nc.psum_tensor("ps_gB", [128, 128], F32))
        ps_mA = e(nc.psum_tensor("ps_mA", [128, 2, 512], F32))
        ps_mT = e(nc.psum_tensor("ps_mT", [128, 2, 512], F32))
        ps_y2 = e(nc.psum_tensor("ps_y2", [128, 2, 512], F32))

        sc = ExitStack()
        ec = sc.enter_context
        w2_bf = ec(nc.sbuf_tensor("w2_bf", [128, 32, D], BF16))
        h1 = ec(nc.sbuf_tensor("h1", [128, 32, CAP], BF16))
        hfbf = ec(nc.sbuf_tensor("hfbf", [128, NT, D], BF16))
        PT = ec(nc.sbuf_tensor("PT", [128, NT, CAP], BF16))
        xinT = ec(nc.sbuf_tensor("xinT", [128, 8, CAP], BF16))
        w1c = ec(nc.sbuf_tensor("w1c", [128, 2, 8, 128], BF16))
        iota640 = ec(nc.sbuf_tensor("iota640", [128, CAP], F32))
        probs_bf = ec(nc.sbuf_tensor("probs_bf", [128, NT, NE], BF16))
        p_sel = ec(nc.sbuf_tensor("p_sel", [128, 8], F32))
        ysel_jit = ec(nc.sbuf_tensor("ysel_jit", [128, 2, D], BF16))
        umat = ec(nc.sbuf_tensor("umat_sb", [128, 128], F32))
        mke = ec(nc.sbuf_tensor("mke", [128, NT], F32))
        zt16 = ec(nc.sbuf_tensor("zt16", [128, NT], F32))
        cse = ec(nc.sbuf_tensor("cse", [128, NT], F32))
        rowt = ec(nc.sbuf_tensor("rowt", [128, 8], F32))
        offs = ec(nc.sbuf_tensor("offs", [128, 8], F32))
        pose = ec(nc.sbuf_tensor("pose", [128, NT], F32))
        vald = ec(nc.sbuf_tensor("vald", [128, NT], F32))

        ld("ld_umat2", umat[:], d_umat[:], after=["mask15"])
        for ex in range(8):
            P.op("gpsimd", (lambda ex=ex: lambda h: h.dma_start(
                out=d_w2bf[HF * ex:HF * (ex + 1), :],
                in_=d_w2[HF * ex:HF * (ex + 1), :]))(),
                after=[f"pc2{ex - 1}"] if ex else [], label=f"pc2{ex}", dma=True)
        P.op("gpsimd", lambda h: h.dma_start(
            out=hfbf[:], in_=d_hfsp[0:2048, :].rearrange("(m p) d -> p m d", p=128)),
            after=["st_hf15", "st_ccp"], label="hfld", dma=True)
        P.op("gpsimd", lambda h: h.iota(iota640[:], pattern=[[1, CAP]], base=0,
             channel_multiplier=0, allow_small_or_imprecise_dtypes=True),
             after=["mask15"], label="iota6")
        P.op("vector", lambda h: h.tensor_copy(probs_bf[:], probs_tm[:]),
             after=["smx15", "mask15"], label="prbf")
        P.op("vector", lambda h: h.memset(zt16[:], 0.0), after=["mask15"], label="zt16")

        for ex in range(8):
            def comp_fixed(h, ex=ex):
                h.tensor_copy(mke[:], mask_tm[:, :, ex])
                h.tensor_tensor_scan(cse[:], zt16[:], mke[:], 0.0, ALU.add, ALU.add)
                return h.tensor_copy(rowt[:, ex:ex + 1], cse[:, NT - 1:NT])
            P.op("vector", comp_fixed, after=["zt16", "mask15"], label=f"cmp{ex}")
        P.op("tensor", lambda h: h.matmul(ps_gB[:, 0:8], umat[:], rowt[:],
             start=True, stop=True),
             after=[f"cmp{ex}" for ex in range(8)] + ["ld_umat2", "tau_rp", "bis_m29"],
             label="offs_mm")
        P.op("vector", lambda h: h.tensor_copy(offs[:], ps_gB[:, 0:8]),
             after=["offs_mm"], label="offs")
        for ex in range(8):
            def posfn(h, ex=ex):
                h.tensor_copy(mke[:], mask_tm[:, :, ex])
                h.tensor_tensor_scan(cse[:], zt16[:], mke[:], 0.0, ALU.add, ALU.add)
                h.tensor_sub(cse[:], cse[:], mke[:])
                h.tensor_scalar(pose[:], cse[:], offs[:, ex:ex + 1], None, ALU.add)
                h.tensor_scalar(vald[:], pose[:], float(CAP), None, ALU.is_lt)
                h.tensor_mul(vald[:], vald[:], mke[:])
                h.tensor_scalar(pose[:], pose[:], 1.0, None, ALU.add)
                h.tensor_mul(pose[:], pose[:], vald[:])
                return h.tensor_scalar(pose_all[:, ex, :], pose[:], -1.0, None, ALU.add)
            P.op("vector", posfn, after=["offs"], label=f"pos{ex}")

        for ex in range(8):
            def ptb(h, ex=ex):
                last = None
                for m in range(NT):
                    last = h.tensor_scalar(PT[:, m, :], iota640[:],
                                           pose_all[:, ex, m:m + 1], None, ALU.is_equal)
                return last
            pta = [f"pos{ex}", "iota6"]
            if ex > 0:
                pta += [f"g{ex - 1}_3", f"psl{ex - 1}"]
            P.op("vector", ptb, after=pta, label=f"PT{ex}")

            for q in range(8):
                gmain = ps_gA[:] if q % 2 == 0 else ps_mA[:, 0, :]
                gtail = ps_gB[:] if q % 2 == 0 else ps_mT[:, 0, 0:128]

                def gfn(h, q=q, gmain=gmain, gtail=gtail):
                    last = None
                    for m in range(NT):
                        h.matmul(gmain, hfbf[:, m, 128 * q:128 * (q + 1)],
                                 PT[:, m, 0:512], start=(m == 0), stop=(m == NT - 1))
                        last = h.matmul(gtail,
                                        hfbf[:, m, 128 * q:128 * (q + 1)],
                                        PT[:, m, 512:CAP],
                                        start=(m == 0), stop=(m == NT - 1))
                    return last
                ga = [f"PT{ex}", "hfld"]
                if q > 1:
                    ga.append(f"gev{ex}_{q - 2}")
                if q == 0:
                    ga.append("offs")
                    if ex > 0:
                        ga += [f"gev{ex - 1}_7", f"pslc{ex - 1}"]
                elif q == 1 and ex > 0:
                    ga += [f"gev{ex - 1}_6", f"gel{ex - 1}_30"]
                P.op("tensor", gfn, after=ga, label=f"g{ex}_{q}")

                def gev(h, q=q, gmain=gmain, gtail=gtail):
                    h.activation(xinT[:, q, 0:512], gmain, ACT.Identity)
                    return h.activation(xinT[:, q, 512:CAP], gtail, ACT.Identity)
                P.op("scalar", gev, after=[f"g{ex}_{q}"], label=f"gev{ex}_{q}")

            def pslf(h, ex=ex):
                last = None
                for c in range(NST):
                    for m in range(NT):
                        last = h.matmul(ps_gB[:, c:c + 1],
                                        PT[:, m, 128 * c:128 * (c + 1)],
                                        probs_bf[:, m, ex:ex + 1],
                                        start=(m == 0), stop=(m == NT - 1))
                return last
            psla = [f"PT{ex}", "prbf", f"g{ex}_7", f"gev{ex}_7"]
            if ex > 0:
                psla.append(f"pslc{ex - 1}")
            P.op("tensor", pslf, after=psla, label=f"psl{ex}")
            pca = [f"psl{ex}"]
            if ex > 0:
                pca += [f"ysc{ex - 1}_{NST - 1}_0", f"ysc{ex - 1}_{NST - 1}_1"]
            P.op("vector", lambda h: h.tensor_copy(p_sel[:, 0:NST], ps_gB[:, 0:NST]),
                 after=pca, label=f"pslc{ex}")

            if ex == 0:
                for ch in range(2):
                    P.op("gpsimd", (lambda ch=ch: lambda h: h.dma_start(
                        out=w1c[:, ch % 2, :, :],
                        in_=d_w1.rearrange("(ee k p) f -> ee p k f", ee=8, p=128)[0][:, :, 128 * ch:128 * (ch + 1)]))(),
                        after=["tau_rep"], label=f"w1l0_{ch}", dma=True)

            for ch in range(32):
                par = ch % 2

                def m1fn(h, ex=ex, ch=ch, par=par):
                    last = None
                    for dt_ in range(8):
                        h.matmul(ps_mA[:, par, :], w1c[:, par, dt_, :],
                                 xinT[:, dt_, 0:512],
                                 start=(dt_ == 0), stop=(dt_ == 7))
                        last = h.matmul(ps_mT[:, par, 0:128], w1c[:, par, dt_, :],
                                        xinT[:, dt_, 512:CAP],
                                        start=(dt_ == 0), stop=(dt_ == 7))
                    return last
                if ch < 31:
                    nw1, nw2_ = ex, ch + 1
                elif ex < 7:
                    nw1, nw2_ = ex + 1, 0
                else:
                    nw1, nw2_ = 7, 31
                mdep = [f"w1l{nw1}_{nw2_}"] + [f"gev{ex}_{qq}" for qq in range(8)]
                if ch >= 2:
                    mdep.append(f"gel{ex}_{ch - 2}")
                elif ex > 0:
                    mdep.append(f"gel{ex - 1}_{30 + ch}")
                P.op("tensor", m1fn, after=mdep, label=f"m1_{ex}_{ch}")
                P.op("scalar", (lambda ex=ex, ch=ch, par=par: lambda h: (
                    h.activation(h1[:, ch, 0:512], ps_mA[:, par, :], ACT.Gelu),
                    h.activation(h1[:, ch, 512:CAP], ps_mT[:, par, 0:128], ACT.Gelu))[-1])(),
                    after=[f"m1_{ex}_{ch}"], label=f"gel{ex}_{ch}")
                nex, nch = ex, ch + 2
                if nch >= 32:
                    nex, nch = ex + 1, nch - 32
                if nex < 8:
                    P.op("gpsimd", (lambda nex=nex, nch=nch: lambda h: h.dma_start(
                        out=w1c[:, nch % 2, :, :],
                        in_=d_w1.rearrange("(ee k p) f -> ee p k f", ee=8, p=128)[nex][:, :, 128 * nch:128 * (nch + 1)]))(),
                        after=[f"m1_{ex}_{ch}"], label=f"w1l{nex}_{nch}", dma=True)

            w2dep = ["tau_rep", f"pc2{ex}"] if ex == 0 else \
                [f"m2_{ex - 1}_{NST - 1}_1", f"pc2{ex}"]
            for qq in range(4):
                P.op("sync", (lambda ex=ex, qq=qq: lambda h: h.dma_start(
                    out=w2_bf[:, 8 * qq:8 * (qq + 1), :],
                    in_=d_w2bf.rearrange("(ee t p) d2 -> ee p t d2", ee=8, p=128)[ex][:, 8 * qq:8 * (qq + 1), :]))(),
                    after=w2dep, label=f"w2l{ex}_{qq}", dma=True)

            gel_all = [f"gel{ex}_{ch}" for ch in range(32)]
            for t in range(NST):
                for dch in range(2):
                    pp = (t * 2 + dch) % 2

                    def m2fn(h, ex=ex, t=t, dch=dch, pp=pp):
                        last = None
                        for ht32 in range(32):
                            last = h.matmul(ps_y2[:, pp, :],
                                            h1[:, ht32, 128 * t:128 * (t + 1)],
                                            w2_bf[:, ht32, 512 * dch:512 * (dch + 1)],
                                            start=(ht32 == 0), stop=(ht32 == 31))
                        return last
                    m2dep = gel_all + [f"w2l{ex}_{qq}" for qq in range(4)]
                    if t > 0 or dch > 0:
                        ptd = (t, dch - 1) if dch else (t - 1, 1)
                        m2dep.append(f"ysc{ex}_{ptd[0]}_{ptd[1]}")
                    elif ex > 0:
                        m2dep.append(f"ysc{ex - 1}_{NST - 1}_1")
                    P.op("tensor", m2fn, after=m2dep, label=f"m2_{ex}_{t}_{dch}")
                    ysca = [f"m2_{ex}_{t}_{dch}", f"pslc{ex}", "rpc_gbb"]
                    if t >= 1:
                        ysca.append(f"yst{ex}_{t - 1}")
                    elif ex > 0:
                        ysca.append(f"yst{ex - 1}_{NST - 1}")
                    P.op("vector", (lambda ex=ex, t=t, dch=dch, pp=pp: lambda h:
                         h.scalar_tensor_tensor(
                             ysel_jit[:, t % 2, 512 * dch:512 * (dch + 1)],
                             ps_y2[:, pp, :], p_sel[:, t:t + 1],
                             gbrep[:, 512 * dch:512 * (dch + 1)],
                             ALU.mult, ALU.mult))(),
                         after=ysca, label=f"ysc{ex}_{t}_{dch}")
                P.op("sync", (lambda ex=ex, t=t: lambda h: h.dma_start(
                    out=d_ysel[CAP * ex + 128 * t:CAP * ex + 128 * (t + 1), :],
                    in_=ysel_jit[:, t % 2, :]))(),
                    after=[f"ysc{ex}_{t}_0", f"ysc{ex}_{t}_1"], label=f"yst{ex}_{t}",
                    dma=True)

        sc.close()
        ysel_all = e(nc.sbuf_tensor("ysel_all", [128, NE, NST, D], BF16))
        y_sb = e(nc.sbuf_tensor("y_sb", [128, NT, D], F32))
        PTT = e(nc.sbuf_tensor("PTT", [128, NST, 1024], BF16))
        pose_bc = e(nc.sbuf_tensor("pose_bc", [128, TPC], F32))
        iota5 = e(nc.sbuf_tensor("iota5", [128, NST], F32))
        onesm = e(nc.sbuf_tensor("onesm", [128, 128], F32))
        pvd = e(nc.sbuf_tensor("pvd", [128, 128], F32))

        fence = ([f"m2_7_{NST - 1}_{dd}" for dd in range(2)]
                 + [f"ysc7_{NST - 1}_{dd}" for dd in range(2)]
                 + [f"yst7_{t}" for t in range(NST)]
                 + ["gel7_31", "pslc7", "gev7_3", "g7_3", "psl7", "PT7"])
        for ex2 in range(8):
            P.op("sync", (lambda ex2=ex2: lambda h: h.dma_start(
                out=ysel_all[:, ex2, :, :],
                in_=d_ysel[CAP * ex2:CAP * (ex2 + 1), :].rearrange(
                    "(c p) d -> p c d", p=128)))(),
                after=fence + [f"yst{ex2}_{t}" for t in range(NST)],
                label=f"ysA{ex2}", dma=True)
        for m in range(NT):
            P.op("sync", (lambda m=m: lambda h: h.dma_start(
                out=y_sb[:, m, :], in_=d_y[128 * m:128 * (m + 1), :]))(),
                after=fence + [f"st_x1_{m}", "ysA7"], label=f"yin{m}", dma=True)
        P.op("gpsimd", lambda h: h.iota(iota5[:], pattern=[[128, NST]], base=0,
             channel_multiplier=1, allow_small_or_imprecise_dtypes=True),
             after=fence, label="iota5")
        P.op("gpsimd", lambda h: h.memset(onesm[:], 1.0), after=fence, label="onesm")

        ysA_all = [f"ysA{ex2}" for ex2 in range(8)]
        for ex in range(8):
            for m in range(NT):
                pva = [f"pos{ex}", "onesm", "iota5"]
                if m >= 1:
                    pva.append(f"bcm{ex}_{m - 1}")
                elif ex > 0:
                    pva.append(f"bcm{ex - 1}_{NT - 1}")
                else:
                    pva += fence
                P.op("vector", (lambda ex=ex, m=m: lambda h: h.tensor_scalar(
                    pvd[:, :], ident[:], pose_all[:, ex, m:m + 1], None, ALU.mult))(),
                    after=pva, label=f"pv{ex}_{m}")
                P.op("tensor", (lambda m=m: lambda h: h.matmul(
                    ps_mT[:, m % 2, 0:128], onesm[:], pvd[:], start=True, stop=True))(),
                    after=[f"pv{ex}_{m}"] + ([f"bcc{ex}_{m - 2}"] if m >= 2 else []) +
                    ([f"bcc{ex - 1}_{14 + m}"] if ex > 0 and m < 2 else []),
                    label=f"bcm{ex}_{m}")
                P.op("vector", (lambda ex=ex, m=m: lambda h: h.tensor_copy(
                    pose_bc[:, 128 * m:128 * (m + 1)], ps_mT[:, m % 2, 0:128]))(),
                    after=[f"bcm{ex}_{m}"], label=f"bcc{ex}_{m}")
            for hh in range(2):
                def pttf(h, ex=ex, hh=hh):
                    last = None
                    for c in range(NST):
                        last = h.tensor_scalar(PTT[:, c, :],
                                               pose_bc[:, 1024 * hh:1024 * (hh + 1)],
                                               iota5[:, c:c + 1], None, ALU.is_equal)
                    return last
                ptta = [f"bcc{ex}_{m}" for m in range(8 * hh, 8 * (hh + 1))] + ["iota5"]
                if hh == 1:
                    ptta.append(f"sc{ex}_7")
                elif ex > 0:
                    ptta.append(f"sc{ex - 1}_15")
                P.op("vector", pttf, after=ptta, label=f"ptt{ex}_{hh}")
                for mi in range(8):
                    m = 8 * hh + mi
                    psA = ps_mA if m % 2 == 0 else ps_y2

                    def scf(h, ex=ex, m=m, mi=mi, psA=psA):
                        last = None
                        for dch in range(2):
                            for c in range(NST):
                                last = h.matmul(psA[:, dch, :],
                                                PTT[:, c, 128 * mi:128 * (mi + 1)],
                                                ysel_all[:, ex, c, 512 * dch:512 * (dch + 1)],
                                                start=(c == 0), stop=(c == NST - 1))
                        return last
                    sca = [f"ptt{ex}_{hh}", "yin15"]
                    if ex > 0:
                        sca.append(f"ya{ex - 1}_{m}")
                    if m >= 2:
                        sca.append(f"ya{ex}_{m - 2}")
                    elif ex > 0:
                        sca.append(f"ya{ex - 1}_{14 + m}")
                    P.op("tensor", scf, after=sca, label=f"sc{ex}_{m}")

                    def yaf(h, ex=ex, m=m, psA=psA):
                        h.tensor_add(y_sb[:, m, 0:512], y_sb[:, m, 0:512], psA[:, 0, :])
                        return h.tensor_add(y_sb[:, m, 512:1024], y_sb[:, m, 512:1024],
                                            psA[:, 1, :])
                    P.op("vector", yaf, after=[f"sc{ex}_{m}"], label=f"ya{ex}_{m}")

        yst2_all = []
        for m in range(NT):
            P.op("sync", (lambda m=m: lambda h: h.dma_start(
                out=d_y[128 * m:128 * (m + 1), :], in_=y_sb[:, m, :]))(),
                after=[f"ya7_{m}"], label=f"yst2_{m}", dma=True)
            yst2_all.append(f"yst2_{m}")

        final_waits = [f"st_x1_{m}" for m in range(16)] + ["st_zrow"]
        final_waits += yst2_all
        if dbg == "c":
            probes = [
                ("tau8", tau8[:], ["tau8"]),
                ("pbis", P_bis[:], ["bis_u29"]),
                ("cnt", cnt_t[:], ["bis_u29"]),
                ("probs", probs_tm[:].rearrange("p a b -> p (a b)"), ["mask15"]),
                ("mask", mask_tm[:].rearrange("p a b -> p (a b)"), ["mask15"]),
                ("pose", pose_all[:].rearrange("p a b -> p (a b)"), [f"pos{x}" for x in range(8)]),
                ("psel", p_sel[:], ["pslc7"]),
                ("xinT", xinT[:, 0, :], ["gev7_3"]),
                ("h1d", h1[:, 0, :], ["gel7_31"]),
                ("ysl", ysel_all[:, 0, 0, :], ["ysA0"]),
                ("ysb", y_sb[:, 0, :], ["ya7_0"]),
                ("hfb", hfbf[:].rearrange("p a b -> p (a b)"), ["hfld", "g7_3"]),
                ("ptd", PT[:, 0, :], ["PT7", "g7_3", "psl7"]),
            ]
            for i, (nm, src, dep) in enumerate(probes):
                eng = "gpsimd" if src.dtype != F32 else "sync"
                P.op(eng, (lambda nm=nm, src=src: lambda h: h.dma_start(
                    out=dbg_out[nm], in_=src))(), after=dep,
                    label=f"dbc{i}", dma=True)
                final_waits.append(f"dbc{i}")
        if dbg is True:
            dbgspec = [("projT", dbg_out["projT"], projT[:], ["projfix", "sp2", "bbar", "CHx"]),
                       ("abar", dbg_out["abar"], abar[:], ["bbar", "scanP", "scanH"]),
                       ("bbar", dbg_out["bbar"], bbar[:], ["scanH"]),
                       ("hT0", dbg_out["hT0"], hT[:, 0, :], [last_tpc, "proj_mm3"])]
            for i, (nm, dst, src, dep) in enumerate(dbgspec):
                eng = "gpsimd" if nm == "hT0" else "sync"
                P.op(eng, (lambda dst=dst, src=src: lambda h: h.dma_start(out=dst, in_=src))(),
                     after=dep, label=f"dbg{i}", dma=True)
                final_waits.append(f"dbg{i}")
            final_waits.append("dbgmod")
            final_waits.append("dbghsc")
            final_waits += ["dbgmv0", "dbgrn0a", "dbgrn0", "dbglnw0",
                            "dbgAe", "dbgHe", "dbgca", "dbgsu", "dbgccr", "dbgccin"]

        def fin(h):
            return None
        P.op("gpsimd", fin, after=final_waits)

        P.emit(blk)

    return nc, dbg_out


def stage_inputs(inp):
    x = np.ascontiguousarray(inp["L0_attn_x"], dtype=np.float32)
    maps = []
    for c in range(8):
        b, q = c // 4, c % 4
        lo = 2048 * q
        xm = np.zeros((17 * 128, D), np.float32)
        xm[:2048] = x[b, lo:lo + 2048]
        if lo - 1 >= 0:
            xm[16 * 128 + 0] = x[b, lo - 1]
        if lo + 2048 < 8192:
            xm[16 * 128 + 1] = x[b, lo + 2048]
        if lo + 2049 < 8192:
            xm[16 * 128 + 2] = x[b, lo + 2049]
        consts = np.zeros((1, 8), np.float32)
        consts[0, 0] = 1.0 if q == 0 else 0.0
        consts[0, 1] = 1.0 if q == 3 else 0.0
        consts[0, 2] = 0.0 if q == 3 else 1.0
        onehot = np.zeros((NCH, 8), np.float32)
        onehot[:, c] = 1.0
        t16 = np.zeros((NCH, 128), np.float32)
        for k in range(NCH):
            t16[k, k::16] = 1.0
        rowm = np.zeros((128, 1), np.float32)
        rowm[16 * c:16 * (c + 1)] = 1.0
        cctest = np.zeros((128, 8), np.float32)
        cctest[16 * c:16 * (c + 1)] = 100.0 * (c + 1) + np.arange(8)[None, :]
        t8 = np.zeros((NE, 64), np.float32)
        for k in range(NE):
            t8[k, k::8] = 1.0
        rowm8 = np.zeros((64, 1), np.float32)
        rowm8[8 * c:8 * (c + 1)] = 1.0
        i8 = np.eye(NE, dtype=np.float32)
        ones8 = np.ones((NE, 128), np.float32)
        esel = np.zeros((128, NE), np.float32)
        for m in range(NE):
            esel[64 * b + m, m] = 1.0
        umat = (np.arange(128)[:, None] < np.arange(128)[None, :]).astype(np.float32)
        pidx = np.arange(128)
        gb_ = pidx // 64
        ge_ = pidx % 8
        gmat = ((gb_[:, None] == gb_[None, :]) & (ge_[:, None] == ge_[None, :])).astype(np.float32)
        maps.append({
            "t8": t8,
            "rowm8": rowm8,
            "i8": i8,
            "ones8": ones8,
            "esel": esel,
            "umat": umat,
            "gmat": gmat,

            "t16": t16,
            "rowm": rowm,
            "cctest": cctest,
            "xt_in": xm,
            "convw": np.ascontiguousarray(inp["L0_attn_conv_w"], dtype=np.float32).reshape(D, 4),
            "convb": np.ascontiguousarray(inp["L0_attn_conv_b"], dtype=np.float32).reshape(D, 1),
            "win": np.ascontiguousarray(inp["L0_attn_Win"], dtype=np.float32),
            "bin": np.ascontiguousarray(inp["L0_attn_b_in"], dtype=np.float32).reshape(1, 80),
            "wout": np.ascontiguousarray(inp["L0_attn_Wout"], dtype=np.float32),
            "bout": np.ascontiguousarray(inp["L0_attn_b_out"], dtype=np.float32).reshape(1, D),
            "alog": np.ascontiguousarray(inp["L0_attn_A_log"], dtype=np.float32).reshape(NCH, 1),
            "gatew": np.ascontiguousarray(inp["L0_ffn_gate_w"], dtype=np.float32),
            "cvec": np.ascontiguousarray(inp["L0_c"][b], dtype=np.float32).reshape(1, D),
            "wmod": np.ascontiguousarray(inp["L0_wmod"], dtype=np.float32),
            "w1": np.ascontiguousarray(inp["L0_ffn_w1"], dtype=np.float32).reshape(NE * D, HF),
            "w2": np.ascontiguousarray(inp["L0_ffn_w2"], dtype=np.float32).reshape(NE * HF, D),
            "consts": consts,
            "onehot": onehot,
        })
    return maps


_CACHE = {}


def kernel(**inputs):
    if "nc" not in _CACHE:
        _CACHE["nc"] = build(dbg=False)[0]
    nc = _CACHE["nc"]
    maps = stage_inputs(inputs)
    res = run_bass_kernel_spmd(nc, maps, core_ids=list(range(8)))
    out = np.zeros((2, 8192, D), np.float32)
    for c in range(8):
        b, q = c // 4, c % 4
        out[b, 2048 * q:2048 * (q + 1)] = res.results[c]["y"][:2048]
    return out
